# version 10
# baseline (speedup 1.0000x reference)
import numpy as np
import concourse.bass as bass
import concourse.mybir as mybir
from concourse.bass_utils import run_bass_kernel_spmd
import ml_dtypes

F32 = mybir.dt.float32
BF16 = mybir.dt.bfloat16
I32 = mybir.dt.int32
AF = mybir.ActivationFunctionType
OP = mybir.AluOpType
AX = mybir.AxisListType
BFNP = ml_dtypes.bfloat16

D = 1024
S = 4096
HALF = 2048
NI = 2816
W_LO = 1408
W_WRAP = 896
SHIFT = -30.0


def _bucket(d):
    b = 16 if d > 0 else 0
    rel = abs(d)
    if rel < 8:
        return b + rel
    rif = 8 + int(np.log(max(rel, 1) / 8.0) / np.log(16.0) * 8)
    return b + min(rif, 15)


def _straddle_set(qb):
    js = set(range(max(0, 4 * qb - 1), 4 * qb + 5))
    if qb == 0:
        js.add(31)
    return js


def _bias_tables(rel, s):
    def gather(dvals):
        idx = np.array([_bucket(d) if d is not None else 0 for d in dvals])
        msk = np.array([d is not None for d in dvals])
        out = rel[idx].T.astype(np.float32).copy()
        out[:, ~msk] = 0.0
        return np.ascontiguousarray(out)
    off = S if s == 1 else 0
    flo = gather([dl if abs(dl) <= 640 else None for dl in range(-640, W_LO - 640)])
    fhi = gather([(dl - off) if abs(dl) <= 640 else None for dl in range(-640, W_LO - 640)])
    fwr = gather([(e + 3328 - off) if e <= 767 else None for e in range(W_WRAP)])
    bmap = np.zeros((1, 1024), np.float32)
    for qb in range(4):
        jset = _straddle_set(qb)
        for g2 in range(16):
            for jj in (2 * g2, 2 * g2 + 1):
                if jj in jset:
                    continue
                k0 = 128 * jj
                d_true = (k0 - qb * 512) - (S if (s == 1 and k0 >= HALF) else 0)
                t = 15 if d_true < 0 else 31
                bmap[0, (qb * 16 + g2) * 16:(qb * 16 + g2) * 16 + 16] = rel[t]
                break
    return flo, fhi, fwr, bmap


class _Sem:
    def __init__(self, h):
        self.h = h
        self.n = 0


class _Prog:
    ENGS = ("sync", "vector", "tensor", "scalar", "gpsimd")

    def __init__(self):
        self.stream = {e: [] for e in self.ENGS}
        self.lastwait = {e: {} for e in self.ENGS}

    def emit(self, eng, fn, inc=None, k=1):
        if inc is not None:
            inc.n += k
            self.stream[eng].append(
                lambda e, fn=fn, sh=inc.h, k=k: fn(e).then_inc(sh, k))
            return inc.n
        self.stream[eng].append(fn)
        return None

    def wait(self, eng, sem, val):
        if not val:
            return
        lw = self.lastwait[eng]
        if lw.get(id(sem), 0) >= val:
            return
        lw[id(sem)] = val
        self.stream[eng].append(lambda e, sh=sem.h, v=val: e.wait_ge(sh, v))


DBG_PD = False


def build_program(debug=False):
    nc = bass.Bass("TRN2", target_bir_lowering=False)

    di = {}
    for name, shape, dt in [
        ("x_fm", [D, S], F32),
        ("wq", [D, D], F32), ("wk", [D, D], F32), ("wv", [D, D], F32), ("wo", [D, D], F32),
        ("wg", [NI, D], F32), ("wu", [NI, D], F32), ("wd", [D, NI], F32),
        ("wmod", [D, 6 * D], F32), ("ccol", [128, 8], F32),
        ("flo", [16, W_LO], F32), ("fhi", [16, W_LO], F32), ("fwr", [16, W_WRAP], F32),
        ("bmap", [1, 1024], F32),
        ("revi", [128, 128], BF16), ("onesb", [128, 64], BF16), ("onesf", [128, 128], F32),
    ]:
        di[name] = nc.dram_tensor(name, shape, dt, kind="ExternalInput").ap()
    d_out = nc.dram_tensor("out_fm", [D, HALF], F32, kind="ExternalOutput").ap()
    if debug:
        dbg_h = nc.dram_tensor("dbg_h", [D, S], BF16, kind="ExternalOutput").ap()
        dbg_m = nc.dram_tensor("dbg_m", [128, 48], F32, kind="ExternalOutput").ap()
        dbg_wk = nc.dram_tensor("dbg_wk", [128, 8192], BF16, kind="ExternalOutput").ap()
        dbg_hd = dbg_h.rearrange("(c p) t -> p c t", p=128)
        dbg_wdqv = nc.dram_tensor("dbg_wdqv", [128, 22528], BF16, kind="ExternalOutput").ap()
        dbg_hqv = nc.dram_tensor("dbg_hqv", [128, 11264], BF16, kind="ExternalOutput").ap()
        dbg_pd = nc.dram_tensor("dbg_pd", [128, 512], F32, kind="ExternalOutput").ap()

    scr = {}
    for name, shape, dt in [
        ("K", [D, S], BF16), ("V", [S, 1032], BF16),
        ("Flo", [16, W_LO], BF16), ("Fhi", [16, W_LO], BF16), ("Fwr", [16, W_WRAP], BF16),
        ("x1", [D, HALF], F32), ("h2", [D, HALF], BF16), ("xq1", [HALF, D], BF16),
        ("hmid", [HALF, NI], BF16),
        ("wgq", [NI, D], BF16), ("wuq", [NI, D], BF16), ("wdq", [D, NI], BF16),
    ]:
        scr[name] = nc.dram_tensor("scr_" + name, shape, dt,
                                   kind=("ExternalOutput" if debug else "Internal")).ap()

    from contextlib import ExitStack
    es = ExitStack()

    def sb(name, shape, dt):
        return es.enter_context(nc.sbuf_tensor(name, shape, dt))

    QT = sb("QT", [128, 8 * HALF], BF16)
    OHB = sb("OHB", [128, 8 * HALF], BF16)
    W3 = sb("W3", [128, 24576], BF16)
    XS = sb("XS", [128, 8192], F32)
    HHB = sb("HHB", [128, 8192], BF16)
    XB = sb("XB", [128, 8192], BF16)
    KV = sb("KV", [128, 11264], BF16)
    c_mod = sb("modc", [128, 48], F32)
    c_cv = sb("cvals", [128, 1024], F32)
    c_1f = sb("onesf1", [128, 128], F32)
    c_1b = sb("onesb1", [128, 64], BF16)
    c_rv = sb("revi1", [128, 128], BF16)
    c_sc = sb("sccol", [128, 8], BF16)
    c_scf = sb("sccolf", [128, 8], F32)
    c_k = sb("ck", [128, 2], F32)
    c_tmp = sb("ctmp", [128, 16], F32)

    r_fe = KV[0:16, 2048:3456]
    r_w = KV[0:1, 9216:11264].bitcast(F32)

    PB = es.enter_context(nc.psum_tensor("PB", [128, 2048], F32))
    PB2 = es.enter_context(nc.psum_tensor("PB2", [128, 2048], F32))

    sLD = _Sem(es.enter_context(nc.semaphore("sLD")))
    sSC = _Sem(es.enter_context(nc.semaphore("sSC")))
    sGP = _Sem(es.enter_context(nc.semaphore("sGP")))
    sT = _Sem(es.enter_context(nc.semaphore("sT")))
    sV = _Sem(es.enter_context(nc.semaphore("sV")))
    sA = _Sem(es.enter_context(nc.semaphore("sA")))
    sST = _Sem(es.enter_context(nc.semaphore("sST")))

    P = _Prog()
    E, Wt = P.emit, P.wait

    def dma(q, qs, out, in_, **kw):
        return E(q, lambda e, o=out, i=in_, kw=kw: e.dma_start(o, i, **kw), inc=qs, k=16)

    def dma_t(q, qs, out, in_):
        return E(q, lambda e, o=out, i=in_: e.dma_start(o, i, transpose=True), inc=qs, k=16)

    xfm = di["x_fm"].rearrange("(c p) t -> p c t", p=128)
    Kd = scr["K"].rearrange("(c p) t -> p c t", p=128)
    Kdp = scr["K"].rearrange("(q p) t -> p q t", p=128)
    Vd = scr["V"].rearrange("(a p) c -> p a c", p=128)
    x1d = scr["x1"].rearrange("(c p) t -> p c t", p=128)
    h2d = scr["h2"].rearrange("(c p) t -> p c t", p=128)
    outd = d_out.rearrange("(c p) t -> p c t", p=128)
    xq1d = scr["xq1"].rearrange("(a p) k -> p a k", p=128)
    hmidd = scr["hmid"].rearrange("(a p) k -> p a k", p=128)

    dma("sync", sLD, c_1f[:], di["onesf"])
    dma("sync", sLD, c_1b[:], di["onesb"])
    dma("sync", sLD, c_rv[:], di["revi"])
    t_cc = dma("sync", sLD, c_scf[:], di["ccol"])

    E("vector", lambda e: e.memset(c_k[:, 0:1], SHIFT))
    E("vector", lambda e: e.memset(c_k[:, 1:2], 1e-5), inc=sV)
    t_ck = sV.n

    Wt("scalar", sLD, t_cc)
    E("scalar", lambda e: e.activation(c_scf[:], c_scf[:], AF.Silu), inc=sA)
    t_silu = sA.n
    Wt("vector", sA, t_silu)
    E("vector", lambda e: e.tensor_copy(c_sc[:], c_scf[:]), inc=sV)
    t_scc = sV.n

    t_F = 0
    for src, dst, w in (("flo", "Flo", W_LO), ("fhi", "Fhi", W_LO), ("fwr", "Fwr", W_WRAP)):
        tok = dma("gpsimd", sGP, r_fe[0:16, 0:w], di[src])
        Wt("sync", sGP, tok)
        t_F = dma("sync", sLD, scr[dst][:, 0:w], r_fe[0:16, 0:w])
        Wt("gpsimd", sLD, t_F)

    t_bm = dma("sync", sLD, r_w[0:1, :], di["bmap"])
    Wt("tensor", sLD, t_bm)
    for i in range(2):
        E("tensor", lambda e, i=i: e.matmul(PB2[:, 1024 + 512 * i:1536 + 512 * i],
                                            c_1f[0:1, 0:128], r_w[0:1, 512 * i:512 * (i + 1)],
                                            start=True, stop=True), inc=sT)
    t_cvm = sT.n
    Wt("vector", sT, t_cvm)
    E("vector", lambda e: e.tensor_scalar(out=c_cv[:], in0=PB2[:, 1024:2048],
                                          scalar1=SHIFT, scalar2=None, op0=OP.add), inc=sV)
    t_cval = sV.n

    wmodr = di["wmod"].rearrange("(c p) n -> p c n", p=128)
    wmst2 = [XS[:, 4096:6144], XS[:, 6144:8192]]
    t_wcast2 = [0, 0]
    t_acc = 0
    for third in range(3):
        for ch in range(8):
            i = third * 8 + ch
            q, qs = (("sync", sLD) if i % 2 == 0 else ("scalar", sSC))
            wmst = wmst2[i % 2]
            Wt(q, sV, t_wcast2[i % 2])
            tok = dma(q, qs, wmst, wmodr[:, ch, 2048 * third:2048 * (third + 1)])
            Wt("vector", qs, tok)
            if third or ch:
                Wt("vector", sT, t_mm)
            E("vector", lambda e, wmst=wmst: e.tensor_copy(KV[:, 0:2048], wmst), inc=sV)
            t_wcast2[i % 2] = sV.n
            Wt("tensor", sV, t_wcast2[i % 2])
            Wt("tensor", sV, t_scc)
            Wt("tensor", sV, t_acc)
            for c16 in range(16):
                E("tensor", lambda e, c16=c16, ch=ch: e.matmul(
                    PB[:, c16:c16 + 1], KV[:, 128 * c16:128 * (c16 + 1)],
                    c_sc[:, ch:ch + 1], start=True, stop=True), inc=sT)
            t_mm = sT.n
            Wt("vector", sT, t_mm)
            lo = 16 * third
            if ch == 0:
                E("vector", lambda e, lo=lo: e.tensor_copy(c_mod[:, lo:lo + 16],
                                                           PB[:, 0:16]), inc=sV)
            else:
                E("vector", lambda e, lo=lo: e.tensor_tensor(
                    c_mod[:, lo:lo + 16], c_mod[:, lo:lo + 16], PB[:, 0:16], OP.add), inc=sV)
            t_acc = sV.n
    E("vector", lambda e: e.tensor_scalar(out=c_mod[:, 8:16], in0=c_mod[:, 8:16],
                                          scalar1=1.0, scalar2=None, op0=OP.add))
    E("vector", lambda e: e.tensor_scalar(out=c_mod[:, 32:40], in0=c_mod[:, 32:40],
                                          scalar1=1.0, scalar2=None, op0=OP.add))
    for c0 in (16, 40):
        g = c_mod[:, c0:c0 + 8]
        tmp = c_tmp[:, 0:8]
        s1 = c_tmp[:, 8:16]
        E("vector", lambda e, g=g, tmp=tmp: e.tensor_scalar(
            out=tmp, in0=g, scalar1=-1.0, scalar2=None, op0=OP.mult), inc=sV)
        Wt("vector", sV, sV.n)
        E("vector", lambda e, g=g, tmp=tmp: e.tensor_tensor(tmp, tmp, g, OP.max), inc=sV)
        Wt("vector", sV, sV.n)
        E("vector", lambda e, g=g, tmp=tmp: e.tensor_scalar(
            out=tmp, in0=tmp, scalar1=0.25, scalar2=None, op0=OP.max), inc=sV)
        Wt("vector", sV, sV.n)
        E("vector", lambda e, g=g, tmp=tmp, s1=s1: e.scalar_tensor_tensor(
            s1, g, 0.0, tmp, OP.is_ge, OP.mult), inc=sV)
        Wt("vector", sV, sV.n)
        E("vector", lambda e, g=g, tmp=tmp, s1=s1: e.scalar_tensor_tensor(
            g, s1, 2.0, tmp, OP.mult, OP.subtract), inc=sV)
    t_mod = sV.n
    if debug:
        Wt("sync", sV, t_mod)
        dma("sync", sLD, dbg_m, c_mod[:])

    t_wk = dma("gpsimd", sGP, W3[:, 0:8192].rearrange("p (c n) -> p c n", n=1024), di["wk"].rearrange("(c p) n -> p c n", p=128))
    t_wv = dma("gpsimd", sGP, W3[:, 8192:16384].rearrange("p (c n) -> p c n", n=1024), di["wv"].rearrange("(c p) n -> p c n", p=128))
    t_wq = dma("gpsimd", sGP, W3[:, 16384:24576].rearrange("p (c n) -> p c n", n=1024), di["wq"].rearrange("(c p) n -> p c n", p=128))

    if debug:
        Wt("sync", sGP, t_wk)
        dma("sync", sLD, dbg_wk, W3[:, 0:8192])

    xst = XS[:, 0:4096].rearrange("p (c f) -> p c f", f=512)
    xbf = XB[:, 0:4096].rearrange("p (c f) -> p c f", f=512)
    sqb = XB[:, 4096:8192].rearrange("p (c f) -> p c f", f=512)
    hta2 = [HHB[:, 0:4096].rearrange("p (c f) -> p c f", f=512),
            HHB[:, 4096:8192].rearrange("p (c f) -> p c f", f=512)]
    kbuf = KV[:, 0:4096].rearrange("p (c f) -> p c f", f=512)
    vbuf = KV[:, 4096:8224].rearrange("p (a c) -> p a c", c=1032)
    vbuf_pr = KV[:, 4096:8224].rearrange("p (a pr y) -> p a pr y", pr=8, y=129)
    wkv = W3[:, 0:8192].rearrange("p (c n) -> p c n", n=1024)
    wvv = W3[:, 8192:16384].rearrange("p (c n) -> p c n", n=1024)
    wqv = W3[:, 16384:24576].rearrange("p (c n) -> p c n", n=1024)
    qtv = QT[:].rearrange("p (c t) -> p c t", t=HALF)

    E("vector", lambda e: e.memset(vbuf_pr[:, :, :, 64:65], 1.0), inc=sV)
    t_vones = sV.n

    t_h_hist = [0, 0]
    t_dbg_h = 0
    t_kvq = 0
    t_stats_prev = 0
    t_kst = t_vst = 0
    t_vev = 0
    kq_free = [0, 0]
    t_phAv = 0

    t_lnrows = 0

    def ln_block(xv, hv, statbank, mb, rb, ca_c, sa_c, h_war):
        nonlocal t_lnrows
        st1, st2 = statbank
        E("vector", lambda e: e.tensor_copy(xbf.rearrange("p c f -> p (c f)"),
                                            xv.rearrange("p c f -> p (c f)")))
        E("vector", lambda e: e.tensor_tensor(sqb.rearrange("p c f -> p (c f)"),
                                              xbf.rearrange("p c f -> p (c f)"),
                                              xbf.rearrange("p c f -> p (c f)"),
                                              OP.mult), inc=sV)
        tc = sV.n
        Wt("tensor", sV, tc)
        Wt("tensor", sV, t_lnrows)
        for src, stp in ((xbf, st1), (sqb, st2)):
            for ch in range(8):
                E("tensor", lambda e, src=src, stp=stp, ch=ch: e.matmul(
                    stp, c_1b[:, 0:1], src[:, ch, :],
                    start=(ch == 0), stop=(ch == 7)), inc=sT)
        tst = sT.n
        Wt("vector", sT, tst)
        E("vector", lambda e, st1=st1: e.tensor_scalar(
            out=r_w[0:1, 0:512], in0=st1, scalar1=1.0 / 1024.0, scalar2=None, op0=OP.mult))
        E("vector", lambda e: e.tensor_tensor(
            r_w[0:1, 512:1024], r_w[0:1, 0:512], r_w[0:1, 0:512], OP.mult))
        E("vector", lambda e, st2=st2: e.scalar_tensor_tensor(
            r_w[0:1, 512:1024], st2, 1.0 / 1024.0, r_w[0:1, 512:1024],
            OP.mult, OP.subtract), inc=sV)
        tv1 = sV.n
        t_lnrows = tv1
        Wt("tensor", sV, tv1)
        E("tensor", lambda e, mb=mb: e.matmul(mb, c_1f[0:1, 0:128], r_w[0:1, 0:512],
                                              start=True, stop=True), inc=sT)
        tmb = sT.n
        Wt("scalar", sT, tmb)
        Wt("scalar", sV, tv1)
        Wt("scalar", sV, t_ck)
        E("scalar", lambda e: e.activation(r_w[0:1, 0:512], r_w[0:1, 512:1024],
                                           AF.Sqrt, bias=c_k[0:1, 1:2]), inc=sA)
        tsq = sA.n
        Wt("vector", sA, tsq)
        E("vector", lambda e: e.reciprocal(r_w[0:1, 512:1024], r_w[0:1, 0:512]), inc=sV)
        trc = sV.n
        Wt("tensor", sV, trc)
        E("tensor", lambda e, rb=rb: e.matmul(rb, c_1f[0:1, 0:128], r_w[0:1, 512:1024],
                                              start=True, stop=True), inc=sT)
        trb = sT.n
        Wt("vector", sT, trb)
        Wt("vector", sV, t_mod)
        Wt("vector", sT, h_war)
        for ch in range(8):
            E("vector", lambda e, ch=ch, mb=mb: e.tensor_sub(xv[:, ch, :], xv[:, ch, :], mb))
            E("vector", lambda e, ch=ch, rb=rb: e.tensor_mul(xv[:, ch, :], xv[:, ch, :], rb))
            E("vector", lambda e, ch=ch: e.tensor_scalar(
                out=hv[:, ch, :], in0=xv[:, ch, :],
                scalar1=c_mod[:, ca_c + ch:ca_c + ch + 1],
                scalar2=c_mod[:, sa_c + ch:sa_c + ch + 1],
                op0=OP.mult, op1=OP.add), inc=sV)
        return sV.n

    t_kvq_hist = [0, 0]
    for tt in range(8):
        hb = tt % 2
        hta = hta2[hb]
        Wt("sync", sV, t_h_hist[0])
        t_x = dma("sync", sLD, xst, xfm[:, :, 512 * tt:512 * (tt + 1)])
        Wt("vector", sLD, t_x)
        Wt("vector", sT, t_stats_prev)
        Wt("tensor", sV, t_cval)
        if debug:
            Wt("vector", sLD, t_dbg_h)
        t_h = ln_block(xst, hta, (PB2[0:1, 0:512], PB2[0:1, 512:1024]),
                       PB2[:, 1024:1536], PB2[:, 1536:2048], 8, 0, t_kvq_hist[hb])
        t_h_hist[0] = t_h
        t_stats_prev = sT.n
        if debug:
            Wt("sync", sV, t_h)
            t_dbg_h = dma("sync", sLD, dbg_hd[:, :, 512 * tt:512 * (tt + 1)], hta)

        Wt("tensor", sV, t_h)
        Wt("tensor", sGP, t_wv if tt >= 4 else t_wq)
        Wt("tensor", sV, t_vones)
        kq_bank = [PB[:, 1024:1536], PB[:, 1536:2048]]
        for kvch in range(8):
            bk = kq_bank[kvch % 2]
            Wt("tensor", sV, kq_free[kvch % 2])
            for ch in range(8):
                E("tensor", lambda e, kvch=kvch, ch=ch, bk=bk, hta=hta: e.matmul(
                    bk, wkv[:, ch, 128 * kvch:128 * (kvch + 1)], hta[:, ch, :],
                    start=(ch == 0), stop=(ch == 7)), inc=sT)
            tkm = sT.n
            Wt("vector", sT, tkm)
            Wt("vector", sSC, t_kst)
            Wt("vector", sLD, t_F)
            E("vector", lambda e, kvch=kvch, bk=bk: e.tensor_copy(kbuf[:, kvch, :], bk),
              inc=sV)
            kq_free[kvch % 2] = sV.n
        t_kev = sV.n
        Wt("scalar", sV, t_kev)
        t_kst = dma("scalar", sSC, Kd[:, :, 512 * tt:512 * (tt + 1)], kbuf)
        for tsub in range(4):
            Wt("tensor", sV, t_mod)
            Wt("tensor", sV, t_vev)
            for ch in range(8):
                for nn in range(2):
                    E("tensor", lambda e, tsub=tsub, ch=ch, nn=nn, hta=hta: e.matmul(
                        PB[:, 512 * nn:512 * (nn + 1)],
                        hta[:, ch, 128 * tsub:128 * (tsub + 1)],
                        wvv[:, ch, 512 * nn:512 * (nn + 1)],
                        start=(ch == 0), stop=(ch == 7)), inc=sT)
            tvm = sT.n
            Wt("vector", sT, tvm)
            Wt("vector", sSC, t_vst)
            pbh = PB[:, 0:1024].rearrange("p (pr par y) -> p pr par y", par=2, y=64)
            E("vector", lambda e, tsub=tsub, pbh=pbh: e.tensor_copy(
                vbuf_pr[:, tsub, :, 0:64], pbh[:, :, 0, :]))
            E("vector", lambda e, tsub=tsub, pbh=pbh: e.tensor_copy(
                vbuf_pr[:, tsub, :, 65:129], pbh[:, :, 1, :]), inc=sV)
            t_vev = sV.n
        Wt("scalar", sV, t_vev)
        t_vst = dma("scalar", sSC, Vd[:, 4 * tt:4 * tt + 4, :], vbuf)
        if tt < 4:
            for kvch in range(8):
                bk = kq_bank[kvch % 2]
                Wt("tensor", sV, kq_free[kvch % 2])
                for ch in range(8):
                    E("tensor", lambda e, kvch=kvch, ch=ch, bk=bk, hta=hta: e.matmul(
                        bk, wqv[:, ch, 128 * kvch:128 * (kvch + 1)], hta[:, ch, :],
                        start=(ch == 0), stop=(ch == 7)), inc=sT)
                tqm = sT.n
                Wt("vector", sT, tqm)
                E("vector", lambda e, kvch=kvch, bk=bk, tt=tt: e.tensor_copy(
                    qtv[:, kvch, 512 * tt:512 * (tt + 1)], bk), inc=sV)
                kq_free[kvch % 2] = sV.n
        t_kvq_hist[hb] = sT.n
    t_kvq = sT.n

    t_phAv = sV.n
    t_phAT = sT.n
    t_phA_kst, t_phA_vst = t_kst, t_vst

    wtiles = ([("wg", "wgq", r, 1024) for r in range(22)] +
              [("wu", "wuq", r, 1024) for r in range(22)] +
              [("wd", "wdq", r, NI) for r in range(8)])
    ws_stage = XS[:, 0:1408]
    ws_u = XS[:, 1408:2816]
    ws_C = XS[:, 2816:4224]
    ws_bm = XS[:, 4224:4400]
    ws_sc = XS[:, 4400:4576]
    ws_ri = XS[:, 4576:4752]
    ws_out = XB[:, 0:2816]

    t_wq_ld = t_wq_qd = t_wq_st = t_Wlast = 0

    def quant_sub(src_ap, dst_ap, w, u_ap, C_ap, bm_ap, sc_ap, ri_ap):
        nb = w // 16
        src3 = src_ap.rearrange("p (b k) -> p b k", k=16)
        E("vector", lambda e: e.reduce_max(
            bm_ap[:, 0:nb].rearrange("p (b o) -> p b o", o=1), src3,
            axis=AX.X, apply_absolute_value=True), inc=sV)
        Wt("vector", sV, sV.n)
        E("vector", lambda e: e.tensor_scalar(out=sc_ap[:, 0:nb], in0=bm_ap[:, 0:nb],
                                              scalar1=1.0 / 6.0, scalar2=1e-12,
                                              op0=OP.mult, op1=OP.max), inc=sV)
        Wt("vector", sV, sV.n)
        E("vector", lambda e: e.reciprocal(ri_ap[:, 0:nb], sc_ap[:, 0:nb]), inc=sV)
        Wt("vector", sV, sV.n)
        u3 = u_ap[:, 0:w].rearrange("p (b k) -> p b k", k=16)
        E("vector", lambda e: e.tensor_tensor(
            u3, src3, ri_ap[:, 0:nb].rearrange("p (b o) -> p b o", o=1)
            .broadcast_to([128, nb, 16]), OP.mult))
        E("vector", lambda e: e.tensor_scalar(
            out=C_ap[:, 0:w].bitcast(I32), in0=u_ap[:, 0:w].bitcast(I32),
            scalar1=0x7F800000, scalar2=0x00400000, op0=OP.bitwise_and,
            op1=OP.bitwise_or), inc=sV)
        Wt("vector", sV, sV.n)
        E("vector", lambda e: e.tensor_scalar(
            out=C_ap[:, 0:w], in0=C_ap[:, 0:w], scalar1=1.5, scalar2=4194304.0,
            op0=OP.max, op1=OP.mult))
        E("vector", lambda e: e.tensor_add(u_ap[:, 0:w], u_ap[:, 0:w], C_ap[:, 0:w]))
        E("vector", lambda e: e.tensor_sub(u_ap[:, 0:w], u_ap[:, 0:w], C_ap[:, 0:w]))
        E("vector", lambda e: e.tensor_tensor(
            dst_ap.rearrange("p (b k) -> p b k", k=16), u3,
            sc_ap[:, 0:nb].rearrange("p (b o) -> p b o", o=1).broadcast_to([128, nb, 16]),
            OP.mult), inc=sV)
        return sV.n

    def emit_w_slice(idx0, idx1):
        nonlocal t_wq_ld, t_wq_qd, t_wq_st, t_Wlast
        Wt("gpsimd", sV, t_phAv)
        for it in range(idx0, min(idx1, len(wtiles))):
            srcn, dstn, r, wfull = wtiles[it]
            nh = (wfull + 1407) // 1408
            for hh in range(nh):
                w = min(1408, wfull - 1408 * hh)
                Wt("gpsimd", sV, t_wq_qd)
                src = di[srcn].rearrange("(a p) k -> p a k", p=128)[:, r,
                                                                   1408 * hh:1408 * hh + w]
                t_wq_ld = dma("gpsimd", sGP, ws_stage[:, 0:w], src)
                Wt("vector", sGP, t_wq_ld)
                Wt("vector", sGP, t_wq_st)
                t_wq_qd = quant_sub(ws_stage[:, 0:w], ws_out[:, 0:w], w,
                                    ws_u, ws_C, ws_bm, ws_sc, ws_ri)
                Wt("gpsimd", sV, t_wq_qd)
                dst = scr[dstn].rearrange("(a p) k -> p a k", p=128)[:, r,
                                                                    1408 * hh:1408 * hh + w]
                t_wq_st = dma("gpsimd", sGP, dst, ws_out[:, 0:w])
        t_Wlast = t_wq_st

    wov = W3[:, 0:8192].rearrange("p (c n) -> p c n", n=1024)
    kpair = [W3[:, 8192:12288], W3[:, 12288:16384]]
    slabs = [W3[:, 16384:19200], W3[:, 19200:22016]]
    vpair = HHB[:, 0:5120].rearrange("p (a c) -> p a c", c=160)
    attnT = [HHB[:, 5120:6144], HHB[:, 6144:7168]]
    obuf = HHB[:, 7168:7680]
    rdx = XS[:, 7680:8192]
    ohv = OHB[:].rearrange("p (c t) -> p c t", t=HALF)

    Wt("vector", sT, t_kvq)
    E("vector", lambda e: e.memset(vpair[:, :, 65:96], 0.0), inc=sV)
    t_vz = sV.n

    Wt("gpsimd", sT, t_kvq)
    t_wo = dma("gpsimd", sGP, W3[:, 0:8192].rearrange("p (c n) -> p c n", n=1024), di["wo"].rearrange("(c p) n -> p c n", p=128))

    slots = [PB[:, 0:1024], PB[:, 1024:2048]]
    NSLOT = 2
    slot_free = [0] * NSLOT
    att_free = [0, 0]
    pouts = [[PB2[:, 0:512], PB2[:, 1024:1536]],
             [PB2[:, 512:1024], PB2[:, 1536:2048]]]
    t_OH_pq = [[0, 0], [0, 0]]
    t_kp = [0, 0]
    t_kp_mm = [t_phAT, t_phAT]
    t_vp = 0
    t_vp_mm = t_phAT
    t_sl = [0, 0]
    t_sl_mm = [t_phAT, t_phAT]
    exp_tok = 0
    pending_fin = []
    rdxb = XB[:, 3000:3512]

    def make_fin(par, hch, qb, t_av):
        den_row = 64 if par == 0 else 32
        num0 = 0 if par == 0 else 64
        pob = pouts[par][qb % 2]

        def fin():
            Wt("vector", sT, t_av)
            E("vector", lambda e: e.reciprocal(
                rdx[den_row:den_row + 1, :], pob[den_row:den_row + 1, :]), inc=sV)
            Wt("vector", sV, sV.n)
            E("vector", lambda e: e.tensor_copy(
                rdxb[den_row:den_row + 1, :], rdx[den_row:den_row + 1, :]), inc=sV)
            trd = sV.n
            Wt("scalar", sT, t_av)
            E("scalar", lambda e: e.copy(
                obuf[num0:num0 + 64, 0:512], pob[num0:num0 + 64, :]), inc=sA)
            tob = sA.n
            Wt("tensor", sV, trd)
            Wt("tensor", sA, tob)
            E("tensor", lambda e: e.matmul(
                pob[num0:num0 + 64, :],
                c_1b[den_row:den_row + 1, 0:64], rdxb[den_row:den_row + 1, :],
                start=True, stop=True), inc=sT)
            tbc = sT.n
            Wt("vector", sT, tbc)
            Wt("vector", sA, tob)
            E("vector", lambda e: e.tensor_tensor(
                ohv[num0:num0 + 64, hch, 512 * qb:512 * (qb + 1)],
                obuf[num0:num0 + 64, 0:512],
                pob[num0:num0 + 64, :], OP.mult), inc=sV)
            t_OH_pq[par][qb % 2] = sV.n
        return fin

    n_w = len(wtiles)
    wper = (n_w + 31) // 32
    wchunk = [0]

    for hp in range(8):
        kb = hp % 2
        Wt("sync", sT, t_kp_mm[kb])
        Wt("sync", sSC, t_phA_kst)
        t_kp[kb] = dma("sync", sLD, kpair[kb], Kdp[:, hp, :])
        Wt("sync", sT, t_vp_mm)
        Wt("sync", sSC, t_phA_vst)
        Wt("sync", sV, t_vz)
        dma("sync", sLD, vpair[:, :, 0:65], Vd[:, :, 129 * hp:129 * hp + 65])
        t_vp = dma("sync", sLD, vpair[:, :, 96:160], Vd[:, :, 129 * hp + 65:129 * (hp + 1)])
        for par in range(2):
            h = 2 * hp + par
            Wt("scalar", sT, t_sl_mm[par])
            Wt("scalar", sLD, t_F)
            dma("scalar", sSC, slabs[par][:, 0:1152],
                bass.AP(scr["Flo"].tensor, h * W_LO + 1, [[1, 128], [128, 9], [1, 128]]))
            dma("scalar", sSC, slabs[par][:, 1152:2304],
                bass.AP(scr["Fhi"].tensor, h * W_LO + 1, [[1, 128], [128, 9], [1, 128]]))
            t_sl[par] = dma("scalar", sSC, slabs[par][:, 2304:2816],
                            bass.AP(scr["Fwr"].tensor, h * W_WRAP + 129,
                                    [[1, 128], [128, 4], [1, 128]]))

        hch = hp
        Wt("scalar", sV, t_cval)

        def av_emit(sm, qb):
            pm, pkm = sm % 2, sm // 2
            po = pouts[pm][qb % 2]
            Wt("tensor", sA, exp_tok)
            if pkm == 0:
                Wt("tensor", sV, t_OH_pq[pm][qb % 2])
            for jj2 in (2 * pkm, 2 * pkm + 1):
                E("tensor", lambda e, jj2=jj2, pkm=pkm, pm=pm, po=po:
                  e.matmul(po[0:65, :] if pm == 0 else po[0:128, :],
                           vpair[:, jj2, 0:65] if pm == 0 else vpair[:, jj2, 32:160],
                           attnT[sm % 2][:, 512 * (jj2 - 2 * pkm):
                                         512 * (jj2 - 2 * pkm) + 512],
                           start=(jj2 == 0), stop=(jj2 == 31)), inc=sT)
            att_free[sm % 2] = sT.n

        for qb in range(4):
            jset = _straddle_set(qb)
            Wt("tensor", sLD, t_kp[kb])
            Wt("tensor", sLD, t_vp)
            Wt("tensor", sSC, t_sl[1])
            Wt("tensor", sV, t_phAv)
            t_av_par = [0, 0]
            for step in range(32):
                par = step % 2
                pk = step // 2
                h = 2 * hp + par
                hrow = par * 64
                sl = step % NSLOT
                Wt("tensor", sA, slot_free[sl])
                for jj, off in ((2 * pk, 0), (2 * pk + 1, 512)):
                    pv = slots[sl][:, off:off + 512]
                    strad = jj in jset
                    E("tensor", lambda e, jj=jj, pv=pv, kb=kb, hrow=hrow, hch=hch, qb=qb:
                      e.matmul(pv, kpair[kb][hrow:hrow + 64, 128 * jj:128 * (jj + 1)],
                               qtv[hrow:hrow + 64, hch, 512 * qb:512 * (qb + 1)],
                               start=True, stop=(not strad),
                               tile_position=(hrow, 0)), inc=sT)
                    if strad:
                        for qp in range(4):
                            if jj == 31 and qb == 0:
                                mi = 2304 + 128 * (3 - qp)
                            else:
                                base = 0 if jj < 16 else 1152
                                mi = base + 128 * (jj - 4 * qb - qp + 4)
                            E("tensor", lambda e, pv=pv, par=par, mi=mi, qp=qp:
                              e.matmul(pv[:, 128 * qp:128 * (qp + 1)],
                                       slabs[par][:, mi:mi + 128], c_rv[:],
                                       start=False, stop=(qp == 3)), inc=sT)
                tsc = sT.n
                if step in (1, 2) and pending_fin:
                    pending_fin.pop(0)()
                if step == 3:
                    emit_w_slice(wchunk[0], wchunk[0] + wper)
                    wchunk[0] += wper
                if step >= 1:
                    av_emit(step - 1, qb)
                    if step - 1 == 30:
                        t_av_par[0] = sT.n
                Wt("scalar", sT, tsc)
                Wt("scalar", sT, att_free[step % 2])
                j0, j1 = 2 * pk, 2 * pk + 1
                s0, s1 = j0 in jset, j1 in jset
                calls = ([(0, 1024, s0)] if s0 == s1 else [(0, 512, s0), (512, 1024, s1)])
                for (o0, o1, st_) in calls:
                    bias = c_k[:, 0:1] if st_ else c_cv[:, (qb * 16 + pk) * 16 + h:
                                                        (qb * 16 + pk) * 16 + h + 1]
                    E("scalar", lambda e, ab=step % 2, sl=sl, o0=o0, o1=o1, bias=bias:
                      e.activation(attnT[ab][:, o0:o1], slots[sl][:, o0:o1],
                                   AF.Exp, bias=bias), inc=sA)
                exp_tok = sA.n
                slot_free[sl] = exp_tok
            av_emit(31, qb)
            t_av_par[1] = sT.n
            pending_fin.append(make_fin(0, hch, qb, t_av_par[0]))
            pending_fin.append(make_fin(1, hch, qb, t_av_par[1]))
        t_sl_mm[0] = t_sl_mm[1] = sT.n
        t_kp_mm[kb] = sT.n
        t_vp_mm = sT.n
    while pending_fin:
        pending_fin.pop(0)()
    emit_w_slice(wchunk[0], n_w)

    xown = XS[:, 0:4096].rearrange("p (c f) -> p c f", f=512)
    x1v = XS[:, 4096:8192].rearrange("p (c f) -> p c f", f=512)
    x1bf = XB[:, 0:4096].rearrange("p (c f) -> p c f", f=512)
    sq2 = XB[:, 4096:8192].rearrange("p (c f) -> p c f", f=512)
    h2tm2 = [KV[:, 0:1024], KV[:, 1024:2048]]
    xq1tm2 = [KV[:, 2048:3072], KV[:, 3072:4096]]
    c1u = KV[:, 4096:6144].bitcast(F32)
    c1C = KV[:, 6144:8192].bitcast(F32)
    c1bm = KV[:, 8192:8320].bitcast(F32)
    c1sc = KV[:, 8320:8448].bitcast(F32)
    c1ri = KV[:, 8448:8576].bitcast(F32)
    t_c1q_h = [0, 0]
    t_c1st_h = [0, 0]
    t_h2st = 0
    t_x1st = 0
    t_res = 0
    Wt("tensor", sA, sA.n)

    for qb in range(4):
        Wt("sync", sV, max(max(t_OH_pq[0]), max(t_OH_pq[1])))
        Wt("sync", sV, t_wq_qd)
        Wt("sync", sGP, t_wq_st)
        t_xo = dma("sync", sLD, xown, xfm[:, :, 512 * qb:512 * (qb + 1)])
        res_tok = {0: 0, 1: 0}
        for dm in range(8):
            pob = PB2[:, 1024 + 512 * (dm % 2):1536 + 512 * (dm % 2)]
            Wt("tensor", sV, max(max(t_OH_pq[0]), max(t_OH_pq[1])))
            Wt("tensor", sGP, t_wo)
            Wt("tensor", sV, res_tok[dm % 2])
            for ch in range(8):
                E("tensor", lambda e, dm=dm, ch=ch, qb=qb, pob=pob: e.matmul(
                    pob, wov[:, ch, 128 * dm:128 * (dm + 1)],
                    ohv[:, ch, 512 * qb:512 * (qb + 1)],
                    start=(ch == 0), stop=(ch == 7)), inc=sT)
            tom = sT.n
            Wt("vector", sT, tom)
            Wt("vector", sLD, t_xo)
            E("vector", lambda e, dm=dm, pob=pob: e.scalar_tensor_tensor(
                x1v[:, dm, :], pob, c_mod[:, 16 + dm:17 + dm], xown[:, dm, :],
                OP.mult, OP.add), inc=sV)
            res_tok[dm % 2] = sV.n
        t_res = sV.n
        Wt("sync", sV, t_res)
        t_x1st = dma("sync", sLD, x1d[:, :, 512 * qb:512 * (qb + 1)], x1v)
        Wt("vector", sLD, t_x1st)
        Wt("vector", sGP, t_wq_st)
        Wt("vector", sLD, t_h2st)
        t_h2 = ln_block(x1v, x1bf, (PB[0:1, 0:512], PB[0:1, 512:1024]),
                        PB[:, 1024:1536], PB[:, 1536:2048], 32, 24, 0)
        Wt("sync", sV, t_h2)
        t_h2st = dma("sync", sLD, h2d[:, :, 512 * qb:512 * (qb + 1)], x1bf)
        for j in range(4):
            tch = 4 * qb + j
            b = j % 2
            Wt("sync", sLD, t_h2st)
            Wt("sync", sV, t_c1q_h[b])
            Wt("sync", sSC, t_phA_vst)
            tld = dma_t("sync", sLD, h2tm2[b], scr["h2"][:, 128 * tch:128 * (tch + 1)])
            Wt("vector", sLD, tld)
            Wt("vector", sSC, t_c1st_h[b])
            t_c1q_h[b] = quant_sub(h2tm2[b], xq1tm2[b], 1024, c1u, c1C, c1bm, c1sc, c1ri)
            Wt("scalar", sV, t_c1q_h[b])
            t_c1st_h[b] = dma("scalar", sSC, xq1d[:, tch, :], xq1tm2[b])

    t_Olast = sT.n
    t_c1q = max(t_c1q_h)
    t_c1st = max(t_c1st_h)

    t_xq1fm = 0
    for ch in range(8):
        Wt("sync", sT, t_Olast)
        Wt("sync", sV, max(max(t_OH_pq[0]), max(t_OH_pq[1])))
        Wt("sync", sSC, t_c1st)
        t_xq1fm = dma_t("sync", sLD, ohv[:, ch, :], scr["xq1"][:, 128 * ch:128 * (ch + 1)])
    cu = XS[:, 0:1024]
    cC = XS[:, 1024:2048]
    cbm = XB[:, 2560:2912].bitcast(F32)
    csc = XB[:, 2912:3264].bitcast(F32)
    cri = XB[:, 3264:3616].bitcast(F32)

    gslab2 = [KV[:, 0:4096].rearrange("p (c f) -> p c f", f=512),
              W3[:, 0:4096].rearrange("p (c f) -> p c f", f=512)]
    uslab2 = [KV[:, 4096:8192].rearrange("p (c f) -> p c f", f=512),
              W3[:, 4096:8192].rearrange("p (c f) -> p c f", f=512)]
    silu_f2 = [XS[:, 2048:2560], XS[:, 3072:3584]]
    silu_f = silu_f2[0]
    hmid_f = XS[:, 2560:3072]
    hq_o = XB[:, 2048:2560]
    c3bm = XB[:, 3616:3680].bitcast(F32)
    c3sc = XB[:, 3680:3744].bitcast(F32)
    c3ri = XB[:, 3744:3808].bitcast(F32)
    NSLAB = [512, 512, 512, 512, 512, 256]
    t_c3q = 0
    t_c3st = 0
    t_c3st_s = [0] * 6
    t_gm = {0: 0, 1: 0, 2: 0, 3: 0}
    t_mult = {0: 0, 1: 0}
    t_slabmm2 = [t_kvq, t_Olast]
    tu2 = [0, 0]
    for ns in range(6):
        Wn = NSLAB[ns]
        n0 = 512 * ns
        sbuf_i = (ns + 1) % 2
        gslab, uslab = gslab2[sbuf_i], uslab2[sbuf_i]
        for ch in range(8):
            Wt("sync", sT, t_slabmm2[sbuf_i])
            if sbuf_i == 0:
                Wt("sync", sSC, t_phA_vst)
                Wt("sync", sV, t_c1q)
                Wt("sync", sSC, t_c1st)
            Wt("sync", sGP, t_Wlast)
            dma_t("sync", sLD, gslab[:, ch, 0:Wn],
                  scr["wgq"][n0:n0 + Wn, 128 * ch:128 * (ch + 1)])
            tu2[sbuf_i] = dma_t("sync", sLD, uslab[:, ch, 0:Wn],
                                scr["wuq"][n0:n0 + Wn, 128 * ch:128 * (ch + 1)])
        for tch in range(16):
            pgb = PB if tch % 4 < 2 else PB2
            pg = pgb[:, 512 * (tch % 2):512 * (tch % 2) + 512]
            pu = pgb[:, 1024 + 512 * (tch % 2):1536 + 512 * (tch % 2)]
            Wt("tensor", sLD, tu2[sbuf_i])
            Wt("tensor", sLD, t_xq1fm)
            Wt("tensor", sV, t_gm[tch % 4])
            for ch in range(8):
                E("tensor", lambda e, ch=ch, tch=tch, Wn=Wn, pg=pg, gslab=gslab: e.matmul(
                    pg[:, 0:Wn], ohv[:, ch, 128 * tch:128 * (tch + 1)], gslab[:, ch, 0:Wn],
                    start=(ch == 0), stop=(ch == 7)), inc=sT)
            for ch in range(8):
                E("tensor", lambda e, ch=ch, tch=tch, Wn=Wn, pu=pu, uslab=uslab: e.matmul(
                    pu[:, 0:Wn], ohv[:, ch, 128 * tch:128 * (tch + 1)], uslab[:, ch, 0:Wn],
                    start=(ch == 0), stop=(ch == 7)), inc=sT)
            tgm = sT.n
            sf = silu_f2[tch % 2]
            Wt("scalar", sT, tgm)
            Wt("scalar", sV, t_mult[tch % 2])
            E("scalar", lambda e, Wn=Wn, pg=pg, sf=sf: e.activation(
                sf[:, 0:Wn], pg[:, 0:Wn], AF.Silu), inc=sA)
            tsil = sA.n
            Wt("vector", sA, tsil)
            E("vector", lambda e, Wn=Wn, pu=pu, sf=sf: e.tensor_tensor(
                hmid_f[:, 0:Wn], sf[:, 0:Wn], pu[:, 0:Wn], OP.mult), inc=sV)
            t_gm[tch % 4] = sV.n
            t_mult[tch % 2] = sV.n
            Wt("vector", sSC, t_c3st)
            t_c3q = quant_sub(hmid_f[:, 0:Wn], hq_o[:, 0:Wn], Wn, cu, cC, c3bm, c3sc, c3ri)
            Wt("scalar", sV, t_c3q)
            t_c3st = dma("scalar", sSC, hmidd[:, tch, n0:n0 + Wn], hq_o[:, 0:Wn])
        t_slabmm2[sbuf_i] = sT.n
        t_c3st_s[ns] = t_c3st

    wdqv = W3[:, 0:22528].rearrange("p (c n) -> p c n", n=1024)
    hqv2 = [QT[:, 0:11264].rearrange("p (c f) -> p c f", f=512),
            KV[:, 0:11264].rearrange("p (c f) -> p c f", f=512)]
    x2b = [XS[:, 3072:3584], XS[:, 3584:4096]]
    t_wdq = 0
    for nch in range(22):
        Wt("sync", sT, t_slabmm2[1])
        Wt("sync", sGP, t_Wlast)
        t_wdq = dma_t("sync", sLD, wdqv[:, nch, :],
                      scr["wdq"][0:1024, 128 * nch:128 * (nch + 1)])
    t_dn = {0: t_gm[2], 1: t_gm[3]}
    t_hq_war = [t_Olast, t_slabmm2[0]]
    if debug:
        Wt("gpsimd", sLD, t_wdq)
        dma("gpsimd", sGP, dbg_wdqv, W3[:, 0:22528])
    t_x2 = 0
    t_ostb = {0: 0, 1: 0}
    th_b = [0, 0]
    for tc4 in range(4):
        hb = tc4 % 2
        hqv = hqv2[hb]
        for nch in range(22):
            ns_need = nch // 4 if nch < 20 else 5
            Wt("sync", sSC, t_c3st_s[ns_need])
            Wt("sync", sT, t_hq_war[hb])
            th_b[hb] = dma_t("sync", sLD, hqv[:, nch, :],
                             scr["hmid"][512 * tc4:512 * (tc4 + 1),
                                         128 * nch:128 * (nch + 1)])
        Wt("sync", sV, t_res)
        Wt("sync", sV, t_x2)
        t_x1l = dma("sync", sLD, x1v, x1d[:, :, 512 * tc4:512 * (tc4 + 1)])
        if debug and tc4 == 3:
            Wt("gpsimd", sLD, th_b[hb])
            dma("gpsimd", sGP, dbg_hqv, KV[:, 0:11264])
        for hc in range(8):
            pd = PB2[:, 512 * (hc % 2):512 * (hc % 2) + 512]
            Wt("tensor", sLD, th_b[hb])
            Wt("tensor", sLD, t_wdq)
            Wt("tensor", sV, t_dn[hc % 2])
            for nch in range(22):
                E("tensor", lambda e, hc=hc, nch=nch, pd=pd, hqv=hqv: e.matmul(
                    pd, wdqv[:, nch, 128 * hc:128 * (hc + 1)], hqv[:, nch, :],
                    start=(nch == 0), stop=(nch == 21)), inc=sT)
            tdm = sT.n
            if debug and tc4 == 3 and hc == 0 and DBG_PD:
                Wt("scalar", sT, tdm)
                E("scalar", lambda e, pd=pd: e.activation(
                    silu_f[:, 0:512], pd, AF.Copy), inc=sA)
                Wt("gpsimd", sA, sA.n)
                dma("gpsimd", sGP, dbg_pd, silu_f[:, 0:512])
            Wt("vector", sT, tdm)
            Wt("vector", sLD, t_x1l)
            Wt("vector", sST, t_ostb[hc % 2])
            xb2 = x2b[hc % 2]
            E("vector", lambda e, hc=hc, xb2=xb2, pd=pd: e.scalar_tensor_tensor(
                xb2, pd, c_mod[:, 40 + hc:41 + hc], x1v[:, hc, :], OP.mult, OP.add), inc=sV)
            t_dn[hc % 2] = sV.n
            t_x2 = sV.n
            Wt("scalar", sV, t_x2)
            t_ostb[hc % 2] = dma("scalar", sST, outd[:, hc, 512 * tc4:512 * (tc4 + 1)], xb2)
        t_hq_war[hb] = sT.n

    Wt("sync", sST, sST.n)

    with nc.Block() as block:
        @block.sync
        def _(e):
            for fn in P.stream["sync"]:
                fn(e)

        @block.vector
        def _(e):
            for fn in P.stream["vector"]:
                fn(e)

        @block.tensor
        def _(e):
            for fn in P.stream["tensor"]:
                fn(e)

        @block.scalar
        def _(e):
            for fn in P.stream["scalar"]:
                fn(e)

        @block.gpsimd
        def _(e):
            for fn in P.stream["gpsimd"]:
                fn(e)

    es.close()
    return nc


_REVI = np.zeros((128, 128), BFNP)
for _i in range(128):
    _REVI[_i, 127 - _i] = 1.0


def kernel(L0_attn_hidden_states, L0_attn_Wq, L0_attn_Wk, L0_attn_Wv, L0_attn_Wo,
           L0_attn_rel_emb, L0_ffn_w_gate, L0_ffn_w_up, L0_ffn_w_down, L0_c, L0_wmod):
    x = np.ascontiguousarray(np.asarray(L0_attn_hidden_states, dtype=np.float32))
    rel = np.ascontiguousarray(np.asarray(L0_attn_rel_emb, dtype=np.float32))
    shared = {
        "wq": np.ascontiguousarray(np.asarray(L0_attn_Wq, np.float32)),
        "wk": np.ascontiguousarray(np.asarray(L0_attn_Wk, np.float32)),
        "wv": np.ascontiguousarray(np.asarray(L0_attn_Wv, np.float32)),
        "wo": np.ascontiguousarray(np.asarray(L0_attn_Wo, np.float32)),
        "wg": np.ascontiguousarray(np.asarray(L0_ffn_w_gate, np.float32)),
        "wu": np.ascontiguousarray(np.asarray(L0_ffn_w_up, np.float32)),
        "wd": np.ascontiguousarray(np.asarray(L0_ffn_w_down, np.float32)),
        "wmod": np.ascontiguousarray(np.asarray(L0_wmod, np.float32)),
        "revi": _REVI,
        "onesb": np.ones((128, 64), BFNP),
        "onesf": np.ones((128, 128), np.float32),
    }
    cc = np.asarray(L0_c, np.float32)
    halves = {}
    for s in (0, 1):
        flo, fhi, fwr, bmap = _bias_tables(rel, s)
        halves[s] = {"flo": flo, "fhi": fhi, "fwr": fwr, "bmap": bmap}

    in_maps = []
    for core in range(8):
        b, s = core // 2, core % 2
        perm = np.concatenate([np.arange(HALF * s, HALF * (s + 1)),
                               np.arange(HALF * (1 - s), HALF * (2 - s))])
        m = dict(shared)
        m.update(halves[s])
        m["x_fm"] = np.ascontiguousarray(x[b][perm].T)
        m["ccol"] = np.ascontiguousarray(cc[b].reshape(8, 128).T)
        in_maps.append(m)

    nc = build_program()
    res = run_bass_kernel_spmd(nc, in_maps, core_ids=list(range(8)))
    out = np.empty((4, S, D), np.float32)
    for core in range(8):
        b, s = core // 2, core % 2
        out[b, HALF * s:HALF * (s + 1)] = np.asarray(res.results[core]["out_fm"]).T
    return out


# version 12
# speedup vs baseline: 1.0119x; 1.0119x over previous
import numpy as np
import concourse.bass as bass
import concourse.mybir as mybir
from concourse.bass_utils import run_bass_kernel_spmd
import ml_dtypes

F32 = mybir.dt.float32
BF16 = mybir.dt.bfloat16
I32 = mybir.dt.int32
AF = mybir.ActivationFunctionType
OP = mybir.AluOpType
AX = mybir.AxisListType
BFNP = ml_dtypes.bfloat16

D = 1024
S = 4096
HALF = 2048
NI = 2816
W_LO = 1408
W_WRAP = 896
SHIFT = -30.0


def _bucket(d):
    b = 16 if d > 0 else 0
    rel = abs(d)
    if rel < 8:
        return b + rel
    rif = 8 + int(np.log(max(rel, 1) / 8.0) / np.log(16.0) * 8)
    return b + min(rif, 15)


def _straddle_set(qb):
    js = set(range(max(0, 4 * qb - 1), 4 * qb + 5))
    if qb == 0:
        js.add(31)
    return js


def _bias_tables(rel, s):
    def gather(dvals):
        idx = np.array([_bucket(d) if d is not None else 0 for d in dvals])
        msk = np.array([d is not None for d in dvals])
        out = rel[idx].T.astype(np.float32).copy()
        out[:, ~msk] = 0.0
        return np.ascontiguousarray(out)
    off = S if s == 1 else 0
    flo = gather([dl if abs(dl) <= 640 else None for dl in range(-640, W_LO - 640)])
    fhi = gather([(dl - off) if abs(dl) <= 640 else None for dl in range(-640, W_LO - 640)])
    fwr = gather([(e + 3328 - off) if e <= 767 else None for e in range(W_WRAP)])
    bmap = np.zeros((1, 1024), np.float32)
    for qb in range(4):
        jset = _straddle_set(qb)
        for g2 in range(16):
            for jj in (2 * g2, 2 * g2 + 1):
                if jj in jset:
                    continue
                k0 = 128 * jj
                d_true = (k0 - qb * 512) - (S if (s == 1 and k0 >= HALF) else 0)
                t = 15 if d_true < 0 else 31
                bmap[0, (qb * 16 + g2) * 16:(qb * 16 + g2) * 16 + 16] = rel[t]
                break
    return flo, fhi, fwr, bmap


class _Sem:
    def __init__(self, h):
        self.h = h
        self.n = 0


class _Prog:
    ENGS = ("sync", "vector", "tensor", "scalar", "gpsimd")

    def __init__(self):
        self.stream = {e: [] for e in self.ENGS}
        self.lastwait = {e: {} for e in self.ENGS}

    def emit(self, eng, fn, inc=None, k=1):
        if inc is not None:
            inc.n += k
            self.stream[eng].append(
                lambda e, fn=fn, sh=inc.h, k=k: fn(e).then_inc(sh, k))
            return inc.n
        self.stream[eng].append(fn)
        return None

    def wait(self, eng, sem, val):
        if not val:
            return
        lw = self.lastwait[eng]
        if lw.get(id(sem), 0) >= val:
            return
        lw[id(sem)] = val
        self.stream[eng].append(lambda e, sh=sem.h, v=val: e.wait_ge(sh, v))


DBG_PD = False


def build_program(debug=False):
    nc = bass.Bass("TRN2", target_bir_lowering=False)

    di = {}
    for name, shape, dt in [
        ("x_fm", [D, S], F32),
        ("wq", [D, D], F32), ("wk", [D, D], F32), ("wv", [D, D], F32), ("wo", [D, D], F32),
        ("wg", [NI, D], F32), ("wu", [NI, D], F32), ("wd", [D, NI], F32),
        ("wmod", [D, 6 * D], F32), ("ccol", [128, 8], F32),
        ("flo", [16, W_LO], F32), ("fhi", [16, W_LO], F32), ("fwr", [16, W_WRAP], F32),
        ("bmap", [1, 1024], F32),
        ("revi", [128, 128], BF16), ("onesb", [128, 64], BF16), ("onesf", [128, 128], F32),
    ]:
        di[name] = nc.dram_tensor(name, shape, dt, kind="ExternalInput").ap()
    d_out = nc.dram_tensor("out_fm", [D, HALF], F32, kind="ExternalOutput").ap()
    if debug:
        dbg_h = nc.dram_tensor("dbg_h", [D, S], BF16, kind="ExternalOutput").ap()
        dbg_m = nc.dram_tensor("dbg_m", [128, 48], F32, kind="ExternalOutput").ap()
        dbg_wk = nc.dram_tensor("dbg_wk", [128, 8192], BF16, kind="ExternalOutput").ap()
        dbg_hd = dbg_h.rearrange("(c p) t -> p c t", p=128)
        dbg_wdqv = nc.dram_tensor("dbg_wdqv", [128, 22528], BF16, kind="ExternalOutput").ap()
        dbg_hqv = nc.dram_tensor("dbg_hqv", [128, 11264], BF16, kind="ExternalOutput").ap()
        dbg_pd = nc.dram_tensor("dbg_pd", [128, 512], F32, kind="ExternalOutput").ap()

    scr = {}
    for name, shape, dt in [
        ("K", [D, S], BF16), ("V", [S, 1032], BF16),
        ("Flo", [16, W_LO], BF16), ("Fhi", [16, W_LO], BF16), ("Fwr", [16, W_WRAP], BF16),
        ("x1", [D, HALF], F32), ("h2", [D, HALF], BF16), ("xq1", [HALF, D], BF16),
        ("hmid", [HALF, NI], BF16),
        ("wgq", [NI, D], BF16), ("wuq", [NI, D], BF16), ("wdq", [D, NI], BF16),
    ]:
        scr[name] = nc.dram_tensor("scr_" + name, shape, dt,
                                   kind=("ExternalOutput" if debug else "Internal")).ap()

    from contextlib import ExitStack
    es = ExitStack()

    def sb(name, shape, dt):
        return es.enter_context(nc.sbuf_tensor(name, shape, dt))

    QT = sb("QT", [128, 8 * HALF], BF16)
    OHB = sb("OHB", [128, 8 * HALF], BF16)
    W3 = sb("W3", [128, 24576], BF16)
    XS = sb("XS", [128, 8192], F32)
    HHB = sb("HHB", [128, 8192], BF16)
    XB = sb("XB", [128, 8192], BF16)
    KV = sb("KV", [128, 11264], BF16)
    c_mod = sb("modc", [128, 48], F32)
    c_cv = sb("cvals", [128, 1024], F32)
    c_1f = sb("onesf1", [128, 128], F32)
    c_1b = sb("onesb1", [128, 64], BF16)
    c_rv = sb("revi1", [128, 128], BF16)
    c_sc = sb("sccol", [128, 8], BF16)
    c_scf = sb("sccolf", [128, 8], F32)
    c_k = sb("ck", [128, 2], F32)
    c_tmp = sb("ctmp", [128, 16], F32)

    r_fe = KV[0:16, 2048:3456]
    r_w = KV[0:1, 9216:11264].bitcast(F32)

    PB = es.enter_context(nc.psum_tensor("PB", [128, 2048], F32))
    PB2 = es.enter_context(nc.psum_tensor("PB2", [128, 2048], F32))

    sLD = _Sem(es.enter_context(nc.semaphore("sLD")))
    sSC = _Sem(es.enter_context(nc.semaphore("sSC")))
    sGP = _Sem(es.enter_context(nc.semaphore("sGP")))
    sT = _Sem(es.enter_context(nc.semaphore("sT")))
    sV = _Sem(es.enter_context(nc.semaphore("sV")))
    sA = _Sem(es.enter_context(nc.semaphore("sA")))
    sST = _Sem(es.enter_context(nc.semaphore("sST")))

    P = _Prog()
    E, Wt = P.emit, P.wait

    def dma(q, qs, out, in_, **kw):
        return E(q, lambda e, o=out, i=in_, kw=kw: e.dma_start(o, i, **kw), inc=qs, k=16)

    def dma_t(q, qs, out, in_):
        return E(q, lambda e, o=out, i=in_: e.dma_start(o, i, transpose=True), inc=qs, k=16)

    xfm = di["x_fm"].rearrange("(c p) t -> p c t", p=128)
    Kd = scr["K"].rearrange("(c p) t -> p c t", p=128)
    Kdp = scr["K"].rearrange("(q p) t -> p q t", p=128)
    Vd = scr["V"].rearrange("(a p) c -> p a c", p=128)
    x1d = scr["x1"].rearrange("(c p) t -> p c t", p=128)
    h2d = scr["h2"].rearrange("(c p) t -> p c t", p=128)
    outd = d_out.rearrange("(c p) t -> p c t", p=128)
    xq1d = scr["xq1"].rearrange("(a p) k -> p a k", p=128)
    hmidd = scr["hmid"].rearrange("(a p) k -> p a k", p=128)

    dma("sync", sLD, c_1f[:], di["onesf"])
    dma("sync", sLD, c_1b[:], di["onesb"])
    dma("sync", sLD, c_rv[:], di["revi"])
    t_cc = dma("sync", sLD, c_scf[:], di["ccol"])

    E("vector", lambda e: e.memset(c_k[:, 0:1], SHIFT))
    E("vector", lambda e: e.memset(c_k[:, 1:2], 1e-5), inc=sV)
    t_ck = sV.n

    Wt("scalar", sLD, t_cc)
    E("scalar", lambda e: e.activation(c_scf[:], c_scf[:], AF.Silu), inc=sA)
    t_silu = sA.n
    Wt("vector", sA, t_silu)
    E("vector", lambda e: e.tensor_copy(c_sc[:], c_scf[:]), inc=sV)
    t_scc = sV.n

    t_F = 0
    for src, dst, w in (("flo", "Flo", W_LO), ("fhi", "Fhi", W_LO), ("fwr", "Fwr", W_WRAP)):
        tok = dma("gpsimd", sGP, r_fe[0:16, 0:w], di[src])
        Wt("sync", sGP, tok)
        t_F = dma("sync", sLD, scr[dst][:, 0:w], r_fe[0:16, 0:w])
        Wt("gpsimd", sLD, t_F)

    t_bm = dma("sync", sLD, r_w[0:1, :], di["bmap"])
    Wt("tensor", sLD, t_bm)
    for i in range(2):
        E("tensor", lambda e, i=i: e.matmul(PB2[:, 1024 + 512 * i:1536 + 512 * i],
                                            c_1f[0:1, 0:128], r_w[0:1, 512 * i:512 * (i + 1)],
                                            start=True, stop=True), inc=sT)
    t_cvm = sT.n
    Wt("vector", sT, t_cvm)
    E("vector", lambda e: e.tensor_scalar(out=c_cv[:], in0=PB2[:, 1024:2048],
                                          scalar1=SHIFT, scalar2=None, op0=OP.add), inc=sV)
    t_cval = sV.n

    wmodr = di["wmod"].rearrange("(c p) n -> p c n", p=128)
    wmst2 = [XS[:, 4096:6144], XS[:, 6144:8192]]
    t_wcast2 = [0, 0]
    t_acc = 0
    for third in range(3):
        for ch in range(8):
            i = third * 8 + ch
            q, qs = (("sync", sLD) if i % 2 == 0 else ("scalar", sSC))
            wmst = wmst2[i % 2]
            Wt(q, sV, t_wcast2[i % 2])
            tok = dma(q, qs, wmst, wmodr[:, ch, 2048 * third:2048 * (third + 1)])
            Wt("vector", qs, tok)
            if third or ch:
                Wt("vector", sT, t_mm)
            E("vector", lambda e, wmst=wmst: e.tensor_copy(KV[:, 0:2048], wmst), inc=sV)
            t_wcast2[i % 2] = sV.n
            Wt("tensor", sV, t_wcast2[i % 2])
            Wt("tensor", sV, t_scc)
            Wt("tensor", sV, t_acc)
            for c16 in range(16):
                E("tensor", lambda e, c16=c16, ch=ch: e.matmul(
                    PB[:, c16:c16 + 1], KV[:, 128 * c16:128 * (c16 + 1)],
                    c_sc[:, ch:ch + 1], start=True, stop=True), inc=sT)
            t_mm = sT.n
            Wt("vector", sT, t_mm)
            lo = 16 * third
            if ch == 0:
                E("vector", lambda e, lo=lo: e.tensor_copy(c_mod[:, lo:lo + 16],
                                                           PB[:, 0:16]), inc=sV)
            else:
                E("vector", lambda e, lo=lo: e.tensor_tensor(
                    c_mod[:, lo:lo + 16], c_mod[:, lo:lo + 16], PB[:, 0:16], OP.add), inc=sV)
            t_acc = sV.n
    E("vector", lambda e: e.tensor_scalar(out=c_mod[:, 8:16], in0=c_mod[:, 8:16],
                                          scalar1=1.0, scalar2=None, op0=OP.add))
    E("vector", lambda e: e.tensor_scalar(out=c_mod[:, 32:40], in0=c_mod[:, 32:40],
                                          scalar1=1.0, scalar2=None, op0=OP.add))
    for c0 in (16, 40):
        g = c_mod[:, c0:c0 + 8]
        tmp = c_tmp[:, 0:8]
        s1 = c_tmp[:, 8:16]
        E("vector", lambda e, g=g, tmp=tmp: e.tensor_scalar(
            out=tmp, in0=g, scalar1=-1.0, scalar2=None, op0=OP.mult), inc=sV)
        Wt("vector", sV, sV.n)
        E("vector", lambda e, g=g, tmp=tmp: e.tensor_tensor(tmp, tmp, g, OP.max), inc=sV)
        Wt("vector", sV, sV.n)
        E("vector", lambda e, g=g, tmp=tmp: e.tensor_scalar(
            out=tmp, in0=tmp, scalar1=0.25, scalar2=None, op0=OP.max), inc=sV)
        Wt("vector", sV, sV.n)
        E("vector", lambda e, g=g, tmp=tmp, s1=s1: e.scalar_tensor_tensor(
            s1, g, 0.0, tmp, OP.is_ge, OP.mult), inc=sV)
        Wt("vector", sV, sV.n)
        E("vector", lambda e, g=g, tmp=tmp, s1=s1: e.scalar_tensor_tensor(
            g, s1, 2.0, tmp, OP.mult, OP.subtract), inc=sV)
    t_mod = sV.n
    if debug:
        Wt("sync", sV, t_mod)
        dma("sync", sLD, dbg_m, c_mod[:])

    t_wk = dma("gpsimd", sGP, W3[:, 0:8192].rearrange("p (c n) -> p c n", n=1024), di["wk"].rearrange("(c p) n -> p c n", p=128))
    t_wv = dma("gpsimd", sGP, W3[:, 8192:16384].rearrange("p (c n) -> p c n", n=1024), di["wv"].rearrange("(c p) n -> p c n", p=128))
    t_wq = dma("gpsimd", sGP, W3[:, 16384:24576].rearrange("p (c n) -> p c n", n=1024), di["wq"].rearrange("(c p) n -> p c n", p=128))

    if debug:
        Wt("sync", sGP, t_wk)
        dma("sync", sLD, dbg_wk, W3[:, 0:8192])

    xst2 = [XS[:, 0:4096].rearrange("p (c f) -> p c f", f=512),
            XS[:, 4096:8192].rearrange("p (c f) -> p c f", f=512)]
    xbf = XB[:, 0:4096].rearrange("p (c f) -> p c f", f=512)
    sqb = XB[:, 4096:8192].rearrange("p (c f) -> p c f", f=512)
    hta2 = [HHB[:, 0:4096].rearrange("p (c f) -> p c f", f=512),
            HHB[:, 4096:8192].rearrange("p (c f) -> p c f", f=512)]
    kbuf = KV[:, 0:4096].rearrange("p (c f) -> p c f", f=512)
    vbuf = KV[:, 4096:8224].rearrange("p (a c) -> p a c", c=1032)
    vbuf_pr = KV[:, 4096:8224].rearrange("p (a pr y) -> p a pr y", pr=8, y=129)
    wkv = W3[:, 0:8192].rearrange("p (c n) -> p c n", n=1024)
    wvv = W3[:, 8192:16384].rearrange("p (c n) -> p c n", n=1024)
    wqv = W3[:, 16384:24576].rearrange("p (c n) -> p c n", n=1024)
    qtv = QT[:].rearrange("p (c t) -> p c t", t=HALF)

    E("vector", lambda e: e.memset(vbuf_pr[:, :, :, 64:65], 1.0), inc=sV)
    t_vones = sV.n

    t_h_hist = [0, max(t_wcast2)]
    t_dbg_h = 0
    t_kvq = 0
    t_stats_prev = 0
    t_kst = t_vst = 0
    t_vev = 0
    kq_free = [0, 0]
    t_phAv = 0

    t_lnrows = 0

    def ln_block(xv, hv, statbank, mb, rb, ca_c, sa_c, h_war):
        nonlocal t_lnrows
        st1, st2 = statbank
        E("vector", lambda e: e.tensor_copy(xbf.rearrange("p c f -> p (c f)"),
                                            xv.rearrange("p c f -> p (c f)")))
        E("vector", lambda e: e.tensor_tensor(sqb.rearrange("p c f -> p (c f)"),
                                              xbf.rearrange("p c f -> p (c f)"),
                                              xbf.rearrange("p c f -> p (c f)"),
                                              OP.mult), inc=sV)
        tc = sV.n
        Wt("tensor", sV, tc)
        Wt("tensor", sV, t_lnrows)
        for src, stp in ((xbf, st1), (sqb, st2)):
            for ch in range(8):
                E("tensor", lambda e, src=src, stp=stp, ch=ch: e.matmul(
                    stp, c_1b[:, 0:1], src[:, ch, :],
                    start=(ch == 0), stop=(ch == 7)), inc=sT)
        tst = sT.n
        Wt("vector", sT, tst)
        E("vector", lambda e, st1=st1: e.tensor_scalar(
            out=r_w[0:1, 0:512], in0=st1, scalar1=1.0 / 1024.0, scalar2=None, op0=OP.mult))
        E("vector", lambda e: e.tensor_tensor(
            r_w[0:1, 512:1024], r_w[0:1, 0:512], r_w[0:1, 0:512], OP.mult))
        E("vector", lambda e, st2=st2: e.scalar_tensor_tensor(
            r_w[0:1, 512:1024], st2, 1.0 / 1024.0, r_w[0:1, 512:1024],
            OP.mult, OP.subtract), inc=sV)
        tv1 = sV.n
        t_lnrows = tv1
        Wt("tensor", sV, tv1)
        E("tensor", lambda e, mb=mb: e.matmul(mb, c_1f[0:1, 0:128], r_w[0:1, 0:512],
                                              start=True, stop=True), inc=sT)
        tmb = sT.n
        Wt("scalar", sT, tmb)
        Wt("scalar", sV, tv1)
        Wt("scalar", sV, t_ck)
        E("scalar", lambda e: e.activation(r_w[0:1, 0:512], r_w[0:1, 512:1024],
                                           AF.Sqrt, bias=c_k[0:1, 1:2]), inc=sA)
        tsq = sA.n
        Wt("vector", sA, tsq)
        E("vector", lambda e: e.reciprocal(r_w[0:1, 512:1024], r_w[0:1, 0:512]), inc=sV)
        trc = sV.n
        Wt("tensor", sV, trc)
        E("tensor", lambda e, rb=rb: e.matmul(rb, c_1f[0:1, 0:128], r_w[0:1, 512:1024],
                                              start=True, stop=True), inc=sT)
        trb = sT.n
        Wt("vector", sT, trb)
        Wt("vector", sV, t_mod)
        Wt("vector", sT, h_war)
        for ch in range(8):
            E("vector", lambda e, ch=ch, mb=mb: e.tensor_sub(xv[:, ch, :], xv[:, ch, :], mb))
            E("vector", lambda e, ch=ch, rb=rb: e.tensor_mul(xv[:, ch, :], xv[:, ch, :], rb))
            E("vector", lambda e, ch=ch: e.tensor_scalar(
                out=hv[:, ch, :], in0=xv[:, ch, :],
                scalar1=c_mod[:, ca_c + ch:ca_c + ch + 1],
                scalar2=c_mod[:, sa_c + ch:sa_c + ch + 1],
                op0=OP.mult, op1=OP.add), inc=sV)
        return sV.n

    t_kvq_hist = [0, 0]
    for tt in range(8):
        hb = tt % 2
        hta = hta2[hb]
        xst = xst2[hb]
        Wt("sync", sV, t_h_hist[hb])
        t_x = dma("sync", sLD, xst, xfm[:, :, 512 * tt:512 * (tt + 1)])
        Wt("vector", sLD, t_x)
        Wt("vector", sT, t_stats_prev)
        Wt("tensor", sV, t_cval)
        if debug:
            Wt("vector", sLD, t_dbg_h)
        t_h = ln_block(xst, hta, (PB2[0:1, 0:512], PB2[0:1, 512:1024]),
                       PB2[:, 1024:1536], PB2[:, 1536:2048], 8, 0, t_kvq_hist[hb])
        t_h_hist[hb] = t_h
        t_stats_prev = sT.n
        if debug:
            Wt("sync", sV, t_h)
            t_dbg_h = dma("sync", sLD, dbg_hd[:, :, 512 * tt:512 * (tt + 1)], hta)

        Wt("tensor", sV, t_h)
        Wt("tensor", sGP, t_wv if tt >= 4 else t_wq)
        Wt("tensor", sV, t_vones)
        kq_bank = [PB[:, 1024:1536], PB[:, 1536:2048]]
        for kvch in range(8):
            bk = kq_bank[kvch % 2]
            Wt("tensor", sA, kq_free[kvch % 2])
            for ch in range(8):
                E("tensor", lambda e, kvch=kvch, ch=ch, bk=bk, hta=hta: e.matmul(
                    bk, wkv[:, ch, 128 * kvch:128 * (kvch + 1)], hta[:, ch, :],
                    start=(ch == 0), stop=(ch == 7)), inc=sT)
            tkm = sT.n
            Wt("scalar", sT, tkm)
            Wt("scalar", sSC, t_kst)
            Wt("scalar", sLD, t_F)
            E("scalar", lambda e, kvch=kvch, bk=bk: e.activation(
                kbuf[:, kvch, :], bk, AF.Copy), inc=sA)
            kq_free[kvch % 2] = sA.n
        t_kev = sA.n
        Wt("scalar", sA, t_kev)
        t_kst = dma("scalar", sSC, Kd[:, :, 512 * tt:512 * (tt + 1)], kbuf)
        for tsub in range(4):
            Wt("tensor", sV, t_mod)
            Wt("tensor", sV, t_vev)
            for ch in range(8):
                for nn in range(2):
                    E("tensor", lambda e, tsub=tsub, ch=ch, nn=nn, hta=hta: e.matmul(
                        PB[:, 512 * nn:512 * (nn + 1)],
                        hta[:, ch, 128 * tsub:128 * (tsub + 1)],
                        wvv[:, ch, 512 * nn:512 * (nn + 1)],
                        start=(ch == 0), stop=(ch == 7)), inc=sT)
            tvm = sT.n
            Wt("vector", sT, tvm)
            Wt("vector", sSC, t_vst)
            pbh = PB[:, 0:1024].rearrange("p (pr par y) -> p pr par y", par=2, y=64)
            E("vector", lambda e, tsub=tsub, pbh=pbh: e.tensor_copy(
                vbuf_pr[:, tsub, :, 0:64], pbh[:, :, 0, :]))
            E("vector", lambda e, tsub=tsub, pbh=pbh: e.tensor_copy(
                vbuf_pr[:, tsub, :, 65:129], pbh[:, :, 1, :]), inc=sV)
            t_vev = sV.n
        Wt("scalar", sV, t_vev)
        t_vst = dma("scalar", sSC, Vd[:, 4 * tt:4 * tt + 4, :], vbuf)
        if tt < 4:
            for kvch in range(8):
                bk = kq_bank[kvch % 2]
                Wt("tensor", sA, kq_free[kvch % 2])
                for ch in range(8):
                    E("tensor", lambda e, kvch=kvch, ch=ch, bk=bk, hta=hta: e.matmul(
                        bk, wqv[:, ch, 128 * kvch:128 * (kvch + 1)], hta[:, ch, :],
                        start=(ch == 0), stop=(ch == 7)), inc=sT)
                tqm = sT.n
                Wt("scalar", sT, tqm)
                E("scalar", lambda e, kvch=kvch, bk=bk, tt=tt: e.activation(
                    qtv[:, kvch, 512 * tt:512 * (tt + 1)], bk, AF.Copy), inc=sA)
                kq_free[kvch % 2] = sA.n
        t_kvq_hist[hb] = sT.n
    t_kvq = sT.n

    t_phAv = sV.n
    t_phAA = sA.n
    t_phAT = sT.n
    t_phA_kst, t_phA_vst = t_kst, t_vst

    wtiles = ([("wg", "wgq", r, 1024) for r in range(22)] +
              [("wu", "wuq", r, 1024) for r in range(22)] +
              [("wd", "wdq", r, NI) for r in range(8)])
    ws_stage = XS[:, 0:1408]
    ws_u = XS[:, 1408:2816]
    ws_C = XS[:, 2816:4224]
    ws_bm = XS[:, 4224:4400]
    ws_sc = XS[:, 4400:4576]
    ws_ri = XS[:, 4576:4752]
    ws_out = XB[:, 0:2816]

    t_wq_ld = t_wq_qd = t_wq_st = t_Wlast = 0

    def quant_sub(src_ap, dst_ap, w, u_ap, C_ap, bm_ap, sc_ap, ri_ap):
        nb = w // 16
        src3 = src_ap.rearrange("p (b k) -> p b k", k=16)
        E("vector", lambda e: e.reduce_max(
            bm_ap[:, 0:nb].rearrange("p (b o) -> p b o", o=1), src3,
            axis=AX.X, apply_absolute_value=True), inc=sV)
        Wt("vector", sV, sV.n)
        E("vector", lambda e: e.tensor_scalar(out=sc_ap[:, 0:nb], in0=bm_ap[:, 0:nb],
                                              scalar1=1.0 / 6.0, scalar2=1e-12,
                                              op0=OP.mult, op1=OP.max), inc=sV)
        Wt("vector", sV, sV.n)
        E("vector", lambda e: e.reciprocal(ri_ap[:, 0:nb], sc_ap[:, 0:nb]), inc=sV)
        Wt("vector", sV, sV.n)
        u3 = u_ap[:, 0:w].rearrange("p (b k) -> p b k", k=16)
        E("vector", lambda e: e.tensor_tensor(
            u3, src3, ri_ap[:, 0:nb].rearrange("p (b o) -> p b o", o=1)
            .broadcast_to([128, nb, 16]), OP.mult))
        E("vector", lambda e: e.tensor_scalar(
            out=C_ap[:, 0:w].bitcast(I32), in0=u_ap[:, 0:w].bitcast(I32),
            scalar1=0x7F800000, scalar2=0x00400000, op0=OP.bitwise_and,
            op1=OP.bitwise_or), inc=sV)
        Wt("vector", sV, sV.n)
        E("vector", lambda e: e.tensor_scalar(
            out=C_ap[:, 0:w], in0=C_ap[:, 0:w], scalar1=1.5, scalar2=4194304.0,
            op0=OP.max, op1=OP.mult))
        E("vector", lambda e: e.tensor_add(u_ap[:, 0:w], u_ap[:, 0:w], C_ap[:, 0:w]))
        E("vector", lambda e: e.tensor_sub(u_ap[:, 0:w], u_ap[:, 0:w], C_ap[:, 0:w]))
        E("vector", lambda e: e.tensor_tensor(
            dst_ap.rearrange("p (b k) -> p b k", k=16), u3,
            sc_ap[:, 0:nb].rearrange("p (b o) -> p b o", o=1).broadcast_to([128, nb, 16]),
            OP.mult), inc=sV)
        return sV.n

    def emit_w_slice(idx0, idx1):
        nonlocal t_wq_ld, t_wq_qd, t_wq_st, t_Wlast
        Wt("gpsimd", sV, t_phAv)
        for it in range(idx0, min(idx1, len(wtiles))):
            srcn, dstn, r, wfull = wtiles[it]
            nh = (wfull + 1407) // 1408
            for hh in range(nh):
                w = min(1408, wfull - 1408 * hh)
                Wt("gpsimd", sV, t_wq_qd)
                src = di[srcn].rearrange("(a p) k -> p a k", p=128)[:, r,
                                                                   1408 * hh:1408 * hh + w]
                t_wq_ld = dma("gpsimd", sGP, ws_stage[:, 0:w], src)
                Wt("vector", sGP, t_wq_ld)
                Wt("vector", sGP, t_wq_st)
                t_wq_qd = quant_sub(ws_stage[:, 0:w], ws_out[:, 0:w], w,
                                    ws_u, ws_C, ws_bm, ws_sc, ws_ri)
                Wt("gpsimd", sV, t_wq_qd)
                dst = scr[dstn].rearrange("(a p) k -> p a k", p=128)[:, r,
                                                                    1408 * hh:1408 * hh + w]
                t_wq_st = dma("gpsimd", sGP, dst, ws_out[:, 0:w])
        t_Wlast = t_wq_st

    wov = W3[:, 0:8192].rearrange("p (c n) -> p c n", n=1024)
    kpair = [W3[:, 8192:12288], W3[:, 12288:16384]]
    slabs = [W3[:, 16384:19200], W3[:, 19200:22016]]
    vpair = HHB[:, 0:5120].rearrange("p (a c) -> p a c", c=160)
    attnT = [HHB[:, 5120:6144], HHB[:, 6144:7168]]
    obuf = HHB[:, 7168:7680]
    rdx = XS[:, 7680:8192]
    ohv = OHB[:].rearrange("p (c t) -> p c t", t=HALF)

    Wt("vector", sT, t_kvq)
    E("vector", lambda e: e.memset(vpair[:, :, 65:96], 0.0), inc=sV)
    t_vz = sV.n

    Wt("gpsimd", sT, t_kvq)
    t_wo = dma("gpsimd", sGP, W3[:, 0:8192].rearrange("p (c n) -> p c n", n=1024), di["wo"].rearrange("(c p) n -> p c n", p=128))

    slots = [PB[:, 0:1024], PB[:, 1024:2048]]
    NSLOT = 2
    slot_free = [0] * NSLOT
    att_free = [0, 0]
    pouts = [[PB2[:, 0:512], PB2[:, 1024:1536]],
             [PB2[:, 512:1024], PB2[:, 1536:2048]]]
    t_OH_pq = [[0, 0], [0, 0]]
    t_kp = [0, 0]
    t_kp_mm = [t_phAT, t_phAT]
    t_vp = 0
    t_vp_mm = t_phAT
    t_sl = [0, 0]
    t_sl_mm = [t_phAT, t_phAT]
    exp_tok = 0
    pending_fin = []
    rdxb = XB[:, 3000:3512]

    def make_fin(par, hch, qb, t_av):
        den_row = 64 if par == 0 else 32
        num0 = 0 if par == 0 else 64
        pob = pouts[par][qb % 2]

        def fin():
            Wt("vector", sT, t_av)
            E("vector", lambda e: e.reciprocal(
                rdx[den_row:den_row + 1, :], pob[den_row:den_row + 1, :]), inc=sV)
            Wt("vector", sV, sV.n)
            E("vector", lambda e: e.tensor_copy(
                rdxb[den_row:den_row + 1, :], rdx[den_row:den_row + 1, :]), inc=sV)
            trd = sV.n
            Wt("scalar", sT, t_av)
            E("scalar", lambda e: e.copy(
                obuf[num0:num0 + 64, 0:512], pob[num0:num0 + 64, :]), inc=sA)
            tob = sA.n
            Wt("tensor", sV, trd)
            Wt("tensor", sA, tob)
            E("tensor", lambda e: e.matmul(
                pob[num0:num0 + 64, :],
                c_1b[den_row:den_row + 1, 0:64], rdxb[den_row:den_row + 1, :],
                start=True, stop=True), inc=sT)
            tbc = sT.n
            Wt("vector", sT, tbc)
            Wt("vector", sA, tob)
            E("vector", lambda e: e.tensor_tensor(
                ohv[num0:num0 + 64, hch, 512 * qb:512 * (qb + 1)],
                obuf[num0:num0 + 64, 0:512],
                pob[num0:num0 + 64, :], OP.mult), inc=sV)
            t_OH_pq[par][qb % 2] = sV.n
        return fin

    n_w = len(wtiles)
    wper = (n_w + 31) // 32
    wchunk = [0]

    for hp in range(8):
        kb = hp % 2
        Wt("sync", sT, t_kp_mm[kb])
        Wt("sync", sSC, t_phA_kst)
        t_kp[kb] = dma("sync", sLD, kpair[kb], Kdp[:, hp, :])
        Wt("sync", sT, t_vp_mm)
        Wt("sync", sSC, t_phA_vst)
        Wt("sync", sV, t_vz)
        dma("sync", sLD, vpair[:, :, 0:65], Vd[:, :, 129 * hp:129 * hp + 65])
        t_vp = dma("sync", sLD, vpair[:, :, 96:160], Vd[:, :, 129 * hp + 65:129 * (hp + 1)])
        for par in range(2):
            h = 2 * hp + par
            Wt("scalar", sT, t_sl_mm[par])
            Wt("scalar", sLD, t_F)
            dma("scalar", sSC, slabs[par][:, 0:1152],
                bass.AP(scr["Flo"].tensor, h * W_LO + 1, [[1, 128], [128, 9], [1, 128]]))
            dma("scalar", sSC, slabs[par][:, 1152:2304],
                bass.AP(scr["Fhi"].tensor, h * W_LO + 1, [[1, 128], [128, 9], [1, 128]]))
            t_sl[par] = dma("scalar", sSC, slabs[par][:, 2304:2816],
                            bass.AP(scr["Fwr"].tensor, h * W_WRAP + 129,
                                    [[1, 128], [128, 4], [1, 128]]))

        hch = hp
        Wt("scalar", sV, t_cval)

        def av_emit(sm, qb):
            pm, pkm = sm % 2, sm // 2
            po = pouts[pm][qb % 2]
            Wt("tensor", sA, exp_tok)
            if pkm == 0:
                Wt("tensor", sV, t_OH_pq[pm][qb % 2])
            for jj2 in (2 * pkm, 2 * pkm + 1):
                E("tensor", lambda e, jj2=jj2, pkm=pkm, pm=pm, po=po:
                  e.matmul(po[0:65, :] if pm == 0 else po[0:128, :],
                           vpair[:, jj2, 0:65] if pm == 0 else vpair[:, jj2, 32:160],
                           attnT[sm % 2][:, 512 * (jj2 - 2 * pkm):
                                         512 * (jj2 - 2 * pkm) + 512],
                           start=(jj2 == 0), stop=(jj2 == 31)), inc=sT)
            att_free[sm % 2] = sT.n

        for qb in range(4):
            jset = _straddle_set(qb)
            Wt("tensor", sLD, t_kp[kb])
            Wt("tensor", sLD, t_vp)
            Wt("tensor", sSC, t_sl[1])
            Wt("tensor", sV, t_phAv)
            Wt("tensor", sA, t_phAA)
            t_av_par = [0, 0]
            for step in range(32):
                par = step % 2
                pk = step // 2
                h = 2 * hp + par
                hrow = par * 64
                sl = step % NSLOT
                Wt("tensor", sA, slot_free[sl])
                for jj, off in ((2 * pk, 0), (2 * pk + 1, 512)):
                    pv = slots[sl][:, off:off + 512]
                    strad = jj in jset
                    E("tensor", lambda e, jj=jj, pv=pv, kb=kb, hrow=hrow, hch=hch, qb=qb:
                      e.matmul(pv, kpair[kb][hrow:hrow + 64, 128 * jj:128 * (jj + 1)],
                               qtv[hrow:hrow + 64, hch, 512 * qb:512 * (qb + 1)],
                               start=True, stop=(not strad),
                               tile_position=(hrow, 0)), inc=sT)
                    if strad:
                        for qp in range(4):
                            if jj == 31 and qb == 0:
                                mi = 2304 + 128 * (3 - qp)
                            else:
                                base = 0 if jj < 16 else 1152
                                mi = base + 128 * (jj - 4 * qb - qp + 4)
                            E("tensor", lambda e, pv=pv, par=par, mi=mi, qp=qp:
                              e.matmul(pv[:, 128 * qp:128 * (qp + 1)],
                                       slabs[par][:, mi:mi + 128], c_rv[:],
                                       start=False, stop=(qp == 3)), inc=sT)
                tsc = sT.n
                if step in (1, 2) and pending_fin:
                    pending_fin.pop(0)()
                if step == 3:
                    emit_w_slice(wchunk[0], wchunk[0] + wper)
                    wchunk[0] += wper
                if step >= 1:
                    av_emit(step - 1, qb)
                    if step - 1 == 30:
                        t_av_par[0] = sT.n
                Wt("scalar", sT, tsc)
                Wt("scalar", sT, att_free[step % 2])
                j0, j1 = 2 * pk, 2 * pk + 1
                s0, s1 = j0 in jset, j1 in jset
                calls = ([(0, 1024, s0)] if s0 == s1 else [(0, 512, s0), (512, 1024, s1)])
                for (o0, o1, st_) in calls:
                    bias = c_k[:, 0:1] if st_ else c_cv[:, (qb * 16 + pk) * 16 + h:
                                                        (qb * 16 + pk) * 16 + h + 1]
                    E("scalar", lambda e, ab=step % 2, sl=sl, o0=o0, o1=o1, bias=bias:
                      e.activation(attnT[ab][:, o0:o1], slots[sl][:, o0:o1],
                                   AF.Exp, bias=bias), inc=sA)
                exp_tok = sA.n
                slot_free[sl] = exp_tok
            av_emit(31, qb)
            t_av_par[1] = sT.n
            pending_fin.append(make_fin(0, hch, qb, t_av_par[0]))
            pending_fin.append(make_fin(1, hch, qb, t_av_par[1]))
        t_sl_mm[0] = t_sl_mm[1] = sT.n
        t_kp_mm[kb] = sT.n
        t_vp_mm = sT.n
    while pending_fin:
        pending_fin.pop(0)()
    emit_w_slice(wchunk[0], n_w)

    xown = XS[:, 0:4096].rearrange("p (c f) -> p c f", f=512)
    x1v = XS[:, 4096:8192].rearrange("p (c f) -> p c f", f=512)
    x1bf = XB[:, 0:4096].rearrange("p (c f) -> p c f", f=512)
    sq2 = XB[:, 4096:8192].rearrange("p (c f) -> p c f", f=512)
    h2tm2 = [KV[:, 0:1024], KV[:, 1024:2048]]
    xq1tm2 = [KV[:, 2048:3072], KV[:, 3072:4096]]
    c1u = KV[:, 4096:6144].bitcast(F32)
    c1C = KV[:, 6144:8192].bitcast(F32)
    c1bm = KV[:, 8192:8320].bitcast(F32)
    c1sc = KV[:, 8320:8448].bitcast(F32)
    c1ri = KV[:, 8448:8576].bitcast(F32)
    t_c1q_h = [0, 0]
    t_c1st_h = [0, 0]
    t_h2st = 0
    t_x1st = 0
    t_res = 0
    Wt("tensor", sA, sA.n)

    for qb in range(4):
        Wt("sync", sV, max(max(t_OH_pq[0]), max(t_OH_pq[1])))
        Wt("sync", sV, t_wq_qd)
        Wt("sync", sGP, t_wq_st)
        t_xo = dma("sync", sLD, xown, xfm[:, :, 512 * qb:512 * (qb + 1)])
        res_tok = {0: 0, 1: 0}
        for dm in range(8):
            pob = PB2[:, 1024 + 512 * (dm % 2):1536 + 512 * (dm % 2)]
            Wt("tensor", sV, max(max(t_OH_pq[0]), max(t_OH_pq[1])))
            Wt("tensor", sGP, t_wo)
            Wt("tensor", sV, res_tok[dm % 2])
            for ch in range(8):
                E("tensor", lambda e, dm=dm, ch=ch, qb=qb, pob=pob: e.matmul(
                    pob, wov[:, ch, 128 * dm:128 * (dm + 1)],
                    ohv[:, ch, 512 * qb:512 * (qb + 1)],
                    start=(ch == 0), stop=(ch == 7)), inc=sT)
            tom = sT.n
            Wt("vector", sT, tom)
            Wt("vector", sLD, t_xo)
            E("vector", lambda e, dm=dm, pob=pob: e.scalar_tensor_tensor(
                x1v[:, dm, :], pob, c_mod[:, 16 + dm:17 + dm], xown[:, dm, :],
                OP.mult, OP.add), inc=sV)
            res_tok[dm % 2] = sV.n
        t_res = sV.n
        Wt("sync", sV, t_res)
        t_x1st = dma("sync", sLD, x1d[:, :, 512 * qb:512 * (qb + 1)], x1v)
        Wt("vector", sLD, t_x1st)
        Wt("vector", sGP, t_wq_st)
        Wt("vector", sLD, t_h2st)
        t_h2 = ln_block(x1v, x1bf, (PB[0:1, 0:512], PB[0:1, 512:1024]),
                        PB[:, 1024:1536], PB[:, 1536:2048], 32, 24, 0)
        Wt("sync", sV, t_h2)
        t_h2st = dma("sync", sLD, h2d[:, :, 512 * qb:512 * (qb + 1)], x1bf)
        for j in range(4):
            tch = 4 * qb + j
            b = j % 2
            Wt("sync", sLD, t_h2st)
            Wt("sync", sV, t_c1q_h[b])
            Wt("sync", sSC, t_phA_vst)
            tld = dma_t("sync", sLD, h2tm2[b], scr["h2"][:, 128 * tch:128 * (tch + 1)])
            Wt("vector", sLD, tld)
            Wt("vector", sSC, t_c1st_h[b])
            t_c1q_h[b] = quant_sub(h2tm2[b], xq1tm2[b], 1024, c1u, c1C, c1bm, c1sc, c1ri)
            Wt("scalar", sV, t_c1q_h[b])
            t_c1st_h[b] = dma("scalar", sSC, xq1d[:, tch, :], xq1tm2[b])

    t_Olast = sT.n
    t_c1q = max(t_c1q_h)
    t_c1st = max(t_c1st_h)

    t_xq1fm = 0
    for ch in range(8):
        Wt("sync", sT, t_Olast)
        Wt("sync", sV, max(max(t_OH_pq[0]), max(t_OH_pq[1])))
        Wt("sync", sSC, t_c1st)
        t_xq1fm = dma_t("sync", sLD, ohv[:, ch, :], scr["xq1"][:, 128 * ch:128 * (ch + 1)])
    cu = XS[:, 0:1024]
    cC = XS[:, 1024:2048]
    cbm = XB[:, 2560:2912].bitcast(F32)
    csc = XB[:, 2912:3264].bitcast(F32)
    cri = XB[:, 3264:3616].bitcast(F32)

    gslab2 = [KV[:, 0:4096].rearrange("p (c f) -> p c f", f=512),
              W3[:, 0:4096].rearrange("p (c f) -> p c f", f=512)]
    uslab2 = [KV[:, 4096:8192].rearrange("p (c f) -> p c f", f=512),
              W3[:, 4096:8192].rearrange("p (c f) -> p c f", f=512)]
    silu_f2 = [XS[:, 2048:2560], XS[:, 3072:3584]]
    silu_f = silu_f2[0]
    hmid_f = XS[:, 2560:3072]
    hq_o = XB[:, 2048:2560]
    c3bm = XB[:, 3616:3680].bitcast(F32)
    c3sc = XB[:, 3680:3744].bitcast(F32)
    c3ri = XB[:, 3744:3808].bitcast(F32)
    NSLAB = [512, 512, 512, 512, 512, 256]
    t_c3q = 0
    t_c3st = 0
    t_c3st_s = [0] * 6
    t_gm = {0: 0, 1: 0, 2: 0, 3: 0}
    t_mult = {0: 0, 1: 0}
    t_slabmm2 = [t_kvq, t_Olast]
    tu2 = [0, 0]
    for ns in range(6):
        Wn = NSLAB[ns]
        n0 = 512 * ns
        sbuf_i = (ns + 1) % 2
        gslab, uslab = gslab2[sbuf_i], uslab2[sbuf_i]
        for ch in range(8):
            Wt("sync", sT, t_slabmm2[sbuf_i])
            if sbuf_i == 0:
                Wt("sync", sSC, t_phA_vst)
                Wt("sync", sV, t_c1q)
                Wt("sync", sSC, t_c1st)
            Wt("sync", sGP, t_Wlast)
            dma_t("sync", sLD, gslab[:, ch, 0:Wn],
                  scr["wgq"][n0:n0 + Wn, 128 * ch:128 * (ch + 1)])
            tu2[sbuf_i] = dma_t("sync", sLD, uslab[:, ch, 0:Wn],
                                scr["wuq"][n0:n0 + Wn, 128 * ch:128 * (ch + 1)])
        for tch in range(16):
            pgb = PB if tch % 4 < 2 else PB2
            pg = pgb[:, 512 * (tch % 2):512 * (tch % 2) + 512]
            pu = pgb[:, 1024 + 512 * (tch % 2):1536 + 512 * (tch % 2)]
            Wt("tensor", sLD, tu2[sbuf_i])
            Wt("tensor", sLD, t_xq1fm)
            Wt("tensor", sV, t_gm[tch % 4])
            for ch in range(8):
                E("tensor", lambda e, ch=ch, tch=tch, Wn=Wn, pg=pg, gslab=gslab: e.matmul(
                    pg[:, 0:Wn], ohv[:, ch, 128 * tch:128 * (tch + 1)], gslab[:, ch, 0:Wn],
                    start=(ch == 0), stop=(ch == 7)), inc=sT)
            for ch in range(8):
                E("tensor", lambda e, ch=ch, tch=tch, Wn=Wn, pu=pu, uslab=uslab: e.matmul(
                    pu[:, 0:Wn], ohv[:, ch, 128 * tch:128 * (tch + 1)], uslab[:, ch, 0:Wn],
                    start=(ch == 0), stop=(ch == 7)), inc=sT)
            tgm = sT.n
            sf = silu_f2[tch % 2]
            Wt("scalar", sT, tgm)
            Wt("scalar", sV, t_mult[tch % 2])
            E("scalar", lambda e, Wn=Wn, pg=pg, sf=sf: e.activation(
                sf[:, 0:Wn], pg[:, 0:Wn], AF.Silu), inc=sA)
            tsil = sA.n
            Wt("vector", sA, tsil)
            E("vector", lambda e, Wn=Wn, pu=pu, sf=sf: e.tensor_tensor(
                hmid_f[:, 0:Wn], sf[:, 0:Wn], pu[:, 0:Wn], OP.mult), inc=sV)
            t_gm[tch % 4] = sV.n
            t_mult[tch % 2] = sV.n
            Wt("vector", sSC, t_c3st)
            t_c3q = quant_sub(hmid_f[:, 0:Wn], hq_o[:, 0:Wn], Wn, cu, cC, c3bm, c3sc, c3ri)
            Wt("scalar", sV, t_c3q)
            t_c3st = dma("scalar", sSC, hmidd[:, tch, n0:n0 + Wn], hq_o[:, 0:Wn])
        t_slabmm2[sbuf_i] = sT.n
        t_c3st_s[ns] = t_c3st

    wdqv = W3[:, 0:22528].rearrange("p (c n) -> p c n", n=1024)
    hqv2 = [QT[:, 0:11264].rearrange("p (c f) -> p c f", f=512),
            KV[:, 0:11264].rearrange("p (c f) -> p c f", f=512)]
    x2b = [XS[:, 3072:3584], XS[:, 3584:4096]]
    t_wdq = 0
    for nch in range(22):
        Wt("sync", sT, t_slabmm2[1])
        Wt("sync", sGP, t_Wlast)
        t_wdq = dma_t("sync", sLD, wdqv[:, nch, :],
                      scr["wdq"][0:1024, 128 * nch:128 * (nch + 1)])
    t_dn = {0: t_gm[2], 1: t_gm[3]}
    t_hq_war = [t_Olast, t_slabmm2[0]]
    if debug:
        Wt("gpsimd", sLD, t_wdq)
        dma("gpsimd", sGP, dbg_wdqv, W3[:, 0:22528])
    t_x2 = 0
    t_ostb = {0: 0, 1: 0}
    th_b = [0, 0]
    for tc4 in range(4):
        hb = tc4 % 2
        hqv = hqv2[hb]
        for nch in range(22):
            ns_need = nch // 4 if nch < 20 else 5
            Wt("sync", sSC, t_c3st_s[ns_need])
            Wt("sync", sT, t_hq_war[hb])
            th_b[hb] = dma_t("sync", sLD, hqv[:, nch, :],
                             scr["hmid"][512 * tc4:512 * (tc4 + 1),
                                         128 * nch:128 * (nch + 1)])
        Wt("sync", sV, t_res)
        Wt("sync", sV, t_x2)
        t_x1l = dma("sync", sLD, x1v, x1d[:, :, 512 * tc4:512 * (tc4 + 1)])
        if debug and tc4 == 3:
            Wt("gpsimd", sLD, th_b[hb])
            dma("gpsimd", sGP, dbg_hqv, KV[:, 0:11264])
        for hc in range(8):
            pd = PB2[:, 512 * (hc % 2):512 * (hc % 2) + 512]
            Wt("tensor", sLD, th_b[hb])
            Wt("tensor", sLD, t_wdq)
            Wt("tensor", sV, t_dn[hc % 2])
            for nch in range(22):
                E("tensor", lambda e, hc=hc, nch=nch, pd=pd, hqv=hqv: e.matmul(
                    pd, wdqv[:, nch, 128 * hc:128 * (hc + 1)], hqv[:, nch, :],
                    start=(nch == 0), stop=(nch == 21)), inc=sT)
            tdm = sT.n
            if debug and tc4 == 3 and hc == 0 and DBG_PD:
                Wt("scalar", sT, tdm)
                E("scalar", lambda e, pd=pd: e.activation(
                    silu_f[:, 0:512], pd, AF.Copy), inc=sA)
                Wt("gpsimd", sA, sA.n)
                dma("gpsimd", sGP, dbg_pd, silu_f[:, 0:512])
            Wt("vector", sT, tdm)
            Wt("vector", sLD, t_x1l)
            Wt("vector", sST, t_ostb[hc % 2])
            xb2 = x2b[hc % 2]
            E("vector", lambda e, hc=hc, xb2=xb2, pd=pd: e.scalar_tensor_tensor(
                xb2, pd, c_mod[:, 40 + hc:41 + hc], x1v[:, hc, :], OP.mult, OP.add), inc=sV)
            t_dn[hc % 2] = sV.n
            t_x2 = sV.n
            Wt("scalar", sV, t_x2)
            t_ostb[hc % 2] = dma("scalar", sST, outd[:, hc, 512 * tc4:512 * (tc4 + 1)], xb2)
        t_hq_war[hb] = sT.n

    Wt("sync", sST, sST.n)

    with nc.Block() as block:
        @block.sync
        def _(e):
            for fn in P.stream["sync"]:
                fn(e)

        @block.vector
        def _(e):
            for fn in P.stream["vector"]:
                fn(e)

        @block.tensor
        def _(e):
            for fn in P.stream["tensor"]:
                fn(e)

        @block.scalar
        def _(e):
            for fn in P.stream["scalar"]:
                fn(e)

        @block.gpsimd
        def _(e):
            for fn in P.stream["gpsimd"]:
                fn(e)

    es.close()
    return nc


_REVI = np.zeros((128, 128), BFNP)
for _i in range(128):
    _REVI[_i, 127 - _i] = 1.0


def kernel(L0_attn_hidden_states, L0_attn_Wq, L0_attn_Wk, L0_attn_Wv, L0_attn_Wo,
           L0_attn_rel_emb, L0_ffn_w_gate, L0_ffn_w_up, L0_ffn_w_down, L0_c, L0_wmod):
    x = np.ascontiguousarray(np.asarray(L0_attn_hidden_states, dtype=np.float32))
    rel = np.ascontiguousarray(np.asarray(L0_attn_rel_emb, dtype=np.float32))
    shared = {
        "wq": np.ascontiguousarray(np.asarray(L0_attn_Wq, np.float32)),
        "wk": np.ascontiguousarray(np.asarray(L0_attn_Wk, np.float32)),
        "wv": np.ascontiguousarray(np.asarray(L0_attn_Wv, np.float32)),
        "wo": np.ascontiguousarray(np.asarray(L0_attn_Wo, np.float32)),
        "wg": np.ascontiguousarray(np.asarray(L0_ffn_w_gate, np.float32)),
        "wu": np.ascontiguousarray(np.asarray(L0_ffn_w_up, np.float32)),
        "wd": np.ascontiguousarray(np.asarray(L0_ffn_w_down, np.float32)),
        "wmod": np.ascontiguousarray(np.asarray(L0_wmod, np.float32)),
        "revi": _REVI,
        "onesb": np.ones((128, 64), BFNP),
        "onesf": np.ones((128, 128), np.float32),
    }
    cc = np.asarray(L0_c, np.float32)
    halves = {}
    for s in (0, 1):
        flo, fhi, fwr, bmap = _bias_tables(rel, s)
        halves[s] = {"flo": flo, "fhi": fhi, "fwr": fwr, "bmap": bmap}

    in_maps = []
    for core in range(8):
        b, s = core // 2, core % 2
        perm = np.concatenate([np.arange(HALF * s, HALF * (s + 1)),
                               np.arange(HALF * (1 - s), HALF * (2 - s))])
        m = dict(shared)
        m.update(halves[s])
        m["x_fm"] = np.ascontiguousarray(x[b][perm].T)
        m["ccol"] = np.ascontiguousarray(cc[b].reshape(8, 128).T)
        in_maps.append(m)

    nc = build_program()
    res = run_bass_kernel_spmd(nc, in_maps, core_ids=list(range(8)))
    out = np.empty((4, S, D), np.float32)
    for core in range(8):
        b, s = core // 2, core % 2
        out[b, HALF * s:HALF * (s + 1)] = np.asarray(res.results[core]["out_fm"]).T
    return out


# version 13 (final)
# speedup vs baseline: 1.0394x; 1.0272x over previous
import numpy as np
import concourse.bass as bass
import concourse.mybir as mybir
from concourse.bass_utils import run_bass_kernel_spmd
import ml_dtypes

F32 = mybir.dt.float32
BF16 = mybir.dt.bfloat16
I32 = mybir.dt.int32
AF = mybir.ActivationFunctionType
OP = mybir.AluOpType
AX = mybir.AxisListType
BFNP = ml_dtypes.bfloat16

D = 1024
S = 4096
HALF = 2048
NI = 2816
W_LO = 1408
W_WRAP = 896
SHIFT = -30.0


def _bucket(d):
    b = 16 if d > 0 else 0
    rel = abs(d)
    if rel < 8:
        return b + rel
    rif = 8 + int(np.log(max(rel, 1) / 8.0) / np.log(16.0) * 8)
    return b + min(rif, 15)


def _straddle_set(qb):
    js = set(range(max(0, 4 * qb - 1), 4 * qb + 5))
    if qb == 0:
        js.add(31)
    return js


def _bias_tables(rel, s):
    def gather(dvals):
        idx = np.array([_bucket(d) if d is not None else 0 for d in dvals])
        msk = np.array([d is not None for d in dvals])
        out = rel[idx].T.astype(np.float32).copy()
        out[:, ~msk] = 0.0
        return np.ascontiguousarray(out)
    off = S if s == 1 else 0
    flo = gather([dl if abs(dl) <= 640 else None for dl in range(-640, W_LO - 640)])
    fhi = gather([(dl - off) if abs(dl) <= 640 else None for dl in range(-640, W_LO - 640)])
    fwr = gather([(e + 3328 - off) if e <= 767 else None for e in range(W_WRAP)])
    bmap = np.zeros((1, 1024), np.float32)
    for qb in range(4):
        jset = _straddle_set(qb)
        for g2 in range(16):
            for jj in (2 * g2, 2 * g2 + 1):
                if jj in jset:
                    continue
                k0 = 128 * jj
                d_true = (k0 - qb * 512) - (S if (s == 1 and k0 >= HALF) else 0)
                t = 15 if d_true < 0 else 31
                bmap[0, (qb * 16 + g2) * 16:(qb * 16 + g2) * 16 + 16] = rel[t]
                break
    return flo, fhi, fwr, bmap


class _Sem:
    def __init__(self, h):
        self.h = h
        self.n = 0


class _Prog:
    ENGS = ("sync", "vector", "tensor", "scalar", "gpsimd")

    def __init__(self):
        self.stream = {e: [] for e in self.ENGS}
        self.lastwait = {e: {} for e in self.ENGS}

    def emit(self, eng, fn, inc=None, k=1):
        if inc is not None:
            inc.n += k
            self.stream[eng].append(
                lambda e, fn=fn, sh=inc.h, k=k: fn(e).then_inc(sh, k))
            return inc.n
        self.stream[eng].append(fn)
        return None

    def wait(self, eng, sem, val):
        if not val:
            return
        lw = self.lastwait[eng]
        if lw.get(id(sem), 0) >= val:
            return
        lw[id(sem)] = val
        self.stream[eng].append(lambda e, sh=sem.h, v=val: e.wait_ge(sh, v))


DBG_PD = False


def build_program(debug=False):
    nc = bass.Bass("TRN2", target_bir_lowering=False)

    di = {}
    for name, shape, dt in [
        ("x_fm", [D, S], F32),
        ("wq", [D, D], F32), ("wk", [D, D], F32), ("wv", [D, D], F32), ("wo", [D, D], F32),
        ("wg", [NI, D], F32), ("wu", [NI, D], F32), ("wd", [D, NI], F32),
        ("wmod", [D, 6 * D], F32), ("ccol", [128, 8], F32),
        ("flo", [16, W_LO], F32), ("fhi", [16, W_LO], F32), ("fwr", [16, W_WRAP], F32),
        ("bmap", [1, 1024], F32),
        ("revi", [128, 128], BF16), ("onesb", [128, 64], BF16), ("onesf", [128, 128], F32),
    ]:
        di[name] = nc.dram_tensor(name, shape, dt, kind="ExternalInput").ap()
    d_out = nc.dram_tensor("out_fm", [D, HALF], F32, kind="ExternalOutput").ap()
    if debug:
        dbg_h = nc.dram_tensor("dbg_h", [D, S], BF16, kind="ExternalOutput").ap()
        dbg_m = nc.dram_tensor("dbg_m", [128, 48], F32, kind="ExternalOutput").ap()
        dbg_wk = nc.dram_tensor("dbg_wk", [128, 8192], BF16, kind="ExternalOutput").ap()
        dbg_hd = dbg_h.rearrange("(c p) t -> p c t", p=128)
        dbg_wdqv = nc.dram_tensor("dbg_wdqv", [128, 22528], BF16, kind="ExternalOutput").ap()
        dbg_hqv = nc.dram_tensor("dbg_hqv", [128, 11264], BF16, kind="ExternalOutput").ap()
        dbg_pd = nc.dram_tensor("dbg_pd", [128, 512], F32, kind="ExternalOutput").ap()

    scr = {}
    for name, shape, dt in [
        ("K", [D, S], BF16), ("V", [S, 1032], BF16),
        ("Flo", [16, W_LO], BF16), ("Fhi", [16, W_LO], BF16), ("Fwr", [16, W_WRAP], BF16),
        ("x1", [D, HALF], F32), ("h2", [D, HALF], BF16), ("xq1", [HALF, D], BF16),
        ("hmid", [HALF, NI], BF16),
        ("wgq", [NI, D], BF16), ("wuq", [NI, D], BF16), ("wdq", [D, NI], BF16),
    ]:
        scr[name] = nc.dram_tensor("scr_" + name, shape, dt,
                                   kind=("ExternalOutput" if debug else "Internal")).ap()

    from contextlib import ExitStack
    es = ExitStack()

    def sb(name, shape, dt):
        return es.enter_context(nc.sbuf_tensor(name, shape, dt))

    QT = sb("QT", [128, 8 * HALF], BF16)
    OHB = sb("OHB", [128, 8 * HALF], BF16)
    W3 = sb("W3", [128, 24576], BF16)
    XS = sb("XS", [128, 8192], F32)
    HHB = sb("HHB", [128, 8192], BF16)
    XB = sb("XB", [128, 8192], BF16)
    KV = sb("KV", [128, 11264], BF16)
    c_mod = sb("modc", [128, 48], F32)
    c_cv = sb("cvals", [128, 1024], F32)
    c_1f = sb("onesf1", [128, 128], F32)
    c_1b = sb("onesb1", [128, 64], BF16)
    c_rv = sb("revi1", [128, 128], BF16)
    c_sc = sb("sccol", [128, 8], BF16)
    c_scf = sb("sccolf", [128, 8], F32)
    c_k = sb("ck", [128, 2], F32)
    c_tmp = sb("ctmp", [128, 16], F32)

    r_fe = KV[0:16, 2048:3456]
    r_w = KV[0:1, 9216:11264].bitcast(F32)

    PB = es.enter_context(nc.psum_tensor("PB", [128, 2048], F32))
    PB2 = es.enter_context(nc.psum_tensor("PB2", [128, 2048], F32))

    sLD = _Sem(es.enter_context(nc.semaphore("sLD")))
    sSC = _Sem(es.enter_context(nc.semaphore("sSC")))
    sGP = _Sem(es.enter_context(nc.semaphore("sGP")))
    sT = _Sem(es.enter_context(nc.semaphore("sT")))
    sV = _Sem(es.enter_context(nc.semaphore("sV")))
    sA = _Sem(es.enter_context(nc.semaphore("sA")))
    sST = _Sem(es.enter_context(nc.semaphore("sST")))

    P = _Prog()
    E, Wt = P.emit, P.wait

    def dma(q, qs, out, in_, **kw):
        return E(q, lambda e, o=out, i=in_, kw=kw: e.dma_start(o, i, **kw), inc=qs, k=16)

    def dma_t(q, qs, out, in_):
        return E(q, lambda e, o=out, i=in_: e.dma_start(o, i, transpose=True), inc=qs, k=16)

    xfm = di["x_fm"].rearrange("(c p) t -> p c t", p=128)
    Kd = scr["K"].rearrange("(c p) t -> p c t", p=128)
    Kdp = scr["K"].rearrange("(q p) t -> p q t", p=128)
    Vd = scr["V"].rearrange("(a p) c -> p a c", p=128)
    x1d = scr["x1"].rearrange("(c p) t -> p c t", p=128)
    h2d = scr["h2"].rearrange("(c p) t -> p c t", p=128)
    outd = d_out.rearrange("(c p) t -> p c t", p=128)
    xq1d = scr["xq1"].rearrange("(a p) k -> p a k", p=128)
    hmidd = scr["hmid"].rearrange("(a p) k -> p a k", p=128)

    dma("sync", sLD, c_1f[:], di["onesf"])
    dma("sync", sLD, c_1b[:], di["onesb"])
    dma("sync", sLD, c_rv[:], di["revi"])
    t_cc = dma("sync", sLD, c_scf[:], di["ccol"])

    E("vector", lambda e: e.memset(c_k[:, 0:1], SHIFT))
    E("vector", lambda e: e.memset(c_k[:, 1:2], 1e-5), inc=sV)
    t_ck = sV.n

    Wt("scalar", sLD, t_cc)
    E("scalar", lambda e: e.activation(c_scf[:], c_scf[:], AF.Silu), inc=sA)
    t_silu = sA.n
    Wt("vector", sA, t_silu)
    E("vector", lambda e: e.tensor_copy(c_sc[:], c_scf[:]), inc=sV)
    t_scc = sV.n

    t_F = 0
    for src, dst, w in (("flo", "Flo", W_LO), ("fhi", "Fhi", W_LO), ("fwr", "Fwr", W_WRAP)):
        tok = dma("gpsimd", sGP, r_fe[0:16, 0:w], di[src])
        Wt("sync", sGP, tok)
        t_F = dma("sync", sLD, scr[dst][:, 0:w], r_fe[0:16, 0:w])
        Wt("gpsimd", sLD, t_F)

    t_bm = dma("sync", sLD, r_w[0:1, :], di["bmap"])
    Wt("tensor", sLD, t_bm)
    for i in range(2):
        E("tensor", lambda e, i=i: e.matmul(PB2[:, 1024 + 512 * i:1536 + 512 * i],
                                            c_1f[0:1, 0:128], r_w[0:1, 512 * i:512 * (i + 1)],
                                            start=True, stop=True), inc=sT)
    t_cvm = sT.n
    Wt("vector", sT, t_cvm)
    E("vector", lambda e: e.tensor_scalar(out=c_cv[:], in0=PB2[:, 1024:2048],
                                          scalar1=SHIFT, scalar2=None, op0=OP.add), inc=sV)
    t_cval = sV.n

    wmodr = di["wmod"].rearrange("(c p) n -> p c n", p=128)
    wmst2 = [XS[:, 4096:6144], XS[:, 6144:8192]]
    t_wcast2 = [0, 0]
    t_acc = 0
    for third in range(3):
        for ch in range(8):
            i = third * 8 + ch
            q, qs = (("sync", sLD) if i % 2 == 0 else ("scalar", sSC))
            wmst = wmst2[i % 2]
            Wt(q, sV, t_wcast2[i % 2])
            tok = dma(q, qs, wmst, wmodr[:, ch, 2048 * third:2048 * (third + 1)])
            Wt("vector", qs, tok)
            if third or ch:
                Wt("vector", sT, t_mm)
            E("vector", lambda e, wmst=wmst: e.tensor_copy(KV[:, 0:2048], wmst), inc=sV)
            t_wcast2[i % 2] = sV.n
            Wt("tensor", sV, t_wcast2[i % 2])
            Wt("tensor", sV, t_scc)
            Wt("tensor", sV, t_acc)
            for c16 in range(16):
                E("tensor", lambda e, c16=c16, ch=ch: e.matmul(
                    PB[:, c16:c16 + 1], KV[:, 128 * c16:128 * (c16 + 1)],
                    c_sc[:, ch:ch + 1], start=True, stop=True), inc=sT)
            t_mm = sT.n
            Wt("vector", sT, t_mm)
            lo = 16 * third
            if ch == 0:
                E("vector", lambda e, lo=lo: e.tensor_copy(c_mod[:, lo:lo + 16],
                                                           PB[:, 0:16]), inc=sV)
            else:
                E("vector", lambda e, lo=lo: e.tensor_tensor(
                    c_mod[:, lo:lo + 16], c_mod[:, lo:lo + 16], PB[:, 0:16], OP.add), inc=sV)
            t_acc = sV.n
    E("vector", lambda e: e.tensor_scalar(out=c_mod[:, 8:16], in0=c_mod[:, 8:16],
                                          scalar1=1.0, scalar2=None, op0=OP.add))
    E("vector", lambda e: e.tensor_scalar(out=c_mod[:, 32:40], in0=c_mod[:, 32:40],
                                          scalar1=1.0, scalar2=None, op0=OP.add))
    for c0 in (16, 40):
        g = c_mod[:, c0:c0 + 8]
        tmp = c_tmp[:, 0:8]
        s1 = c_tmp[:, 8:16]
        E("vector", lambda e, g=g, tmp=tmp: e.tensor_scalar(
            out=tmp, in0=g, scalar1=-1.0, scalar2=None, op0=OP.mult), inc=sV)
        Wt("vector", sV, sV.n)
        E("vector", lambda e, g=g, tmp=tmp: e.tensor_tensor(tmp, tmp, g, OP.max), inc=sV)
        Wt("vector", sV, sV.n)
        E("vector", lambda e, g=g, tmp=tmp: e.tensor_scalar(
            out=tmp, in0=tmp, scalar1=0.25, scalar2=None, op0=OP.max), inc=sV)
        Wt("vector", sV, sV.n)
        E("vector", lambda e, g=g, tmp=tmp, s1=s1: e.scalar_tensor_tensor(
            s1, g, 0.0, tmp, OP.is_ge, OP.mult), inc=sV)
        Wt("vector", sV, sV.n)
        E("vector", lambda e, g=g, tmp=tmp, s1=s1: e.scalar_tensor_tensor(
            g, s1, 2.0, tmp, OP.mult, OP.subtract), inc=sV)
    t_mod = sV.n
    if debug:
        Wt("sync", sV, t_mod)
        dma("sync", sLD, dbg_m, c_mod[:])

    t_wk = dma("gpsimd", sGP, W3[:, 0:8192].rearrange("p (c n) -> p c n", n=1024), di["wk"].rearrange("(c p) n -> p c n", p=128))
    t_wv = dma("gpsimd", sGP, W3[:, 8192:16384].rearrange("p (c n) -> p c n", n=1024), di["wv"].rearrange("(c p) n -> p c n", p=128))
    t_wq = dma("gpsimd", sGP, W3[:, 16384:24576].rearrange("p (c n) -> p c n", n=1024), di["wq"].rearrange("(c p) n -> p c n", p=128))

    if debug:
        Wt("sync", sGP, t_wk)
        dma("sync", sLD, dbg_wk, W3[:, 0:8192])

    xst = XS[:, 0:4096].rearrange("p (c f) -> p c f", f=512)
    xbf = XB[:, 0:4096].rearrange("p (c f) -> p c f", f=512)
    sqb = XB[:, 4096:8192].rearrange("p (c f) -> p c f", f=512)
    hta2 = [HHB[:, 0:4096].rearrange("p (c f) -> p c f", f=512),
            HHB[:, 4096:8192].rearrange("p (c f) -> p c f", f=512)]
    kbuf = KV[:, 0:4096].rearrange("p (c f) -> p c f", f=512)
    vbuf = KV[:, 4096:8224].rearrange("p (a c) -> p a c", c=1032)
    vbuf_pr = KV[:, 4096:8224].rearrange("p (a pr y) -> p a pr y", pr=8, y=129)
    wkv = W3[:, 0:8192].rearrange("p (c n) -> p c n", n=1024)
    wvv = W3[:, 8192:16384].rearrange("p (c n) -> p c n", n=1024)
    wqv = W3[:, 16384:24576].rearrange("p (c n) -> p c n", n=1024)
    qtv = QT[:].rearrange("p (c t) -> p c t", t=HALF)

    E("vector", lambda e: e.memset(vbuf_pr[:, :, :, 64:65], 1.0), inc=sV)
    t_vones = sV.n

    t_h_hist = [0, 0]
    t_dbg_h = 0
    t_kvq = 0
    t_stats_prev = 0
    t_kst = t_vst = 0
    t_vev = 0
    kq_free = [0, 0]
    t_phAv = 0

    t_lnrows = 0

    def ln_block(xv, hv, statbank, mb, rb, ca_c, sa_c, h_war):
        nonlocal t_lnrows
        st1, st2 = statbank
        E("vector", lambda e: e.tensor_copy(xbf.rearrange("p c f -> p (c f)"),
                                            xv.rearrange("p c f -> p (c f)")))
        E("vector", lambda e: e.tensor_tensor(sqb.rearrange("p c f -> p (c f)"),
                                              xbf.rearrange("p c f -> p (c f)"),
                                              xbf.rearrange("p c f -> p (c f)"),
                                              OP.mult), inc=sV)
        tc = sV.n
        Wt("tensor", sV, tc)
        Wt("tensor", sV, t_lnrows)
        for src, stp in ((xbf, st1), (sqb, st2)):
            for ch in range(8):
                E("tensor", lambda e, src=src, stp=stp, ch=ch: e.matmul(
                    stp, c_1b[:, 0:1], src[:, ch, :],
                    start=(ch == 0), stop=(ch == 7)), inc=sT)
        tst = sT.n
        Wt("vector", sT, tst)
        E("vector", lambda e, st1=st1: e.tensor_scalar(
            out=r_w[0:1, 0:512], in0=st1, scalar1=1.0 / 1024.0, scalar2=None, op0=OP.mult))
        E("vector", lambda e: e.tensor_tensor(
            r_w[0:1, 512:1024], r_w[0:1, 0:512], r_w[0:1, 0:512], OP.mult))
        E("vector", lambda e, st2=st2: e.scalar_tensor_tensor(
            r_w[0:1, 512:1024], st2, 1.0 / 1024.0, r_w[0:1, 512:1024],
            OP.mult, OP.subtract), inc=sV)
        tv1 = sV.n
        t_lnrows = tv1
        Wt("tensor", sV, tv1)
        E("tensor", lambda e, mb=mb: e.matmul(mb, c_1f[0:1, 0:128], r_w[0:1, 0:512],
                                              start=True, stop=True), inc=sT)
        tmb = sT.n
        Wt("scalar", sT, tmb)
        Wt("scalar", sV, tv1)
        Wt("scalar", sV, t_ck)
        E("scalar", lambda e: e.activation(r_w[0:1, 0:512], r_w[0:1, 512:1024],
                                           AF.Sqrt, bias=c_k[0:1, 1:2]), inc=sA)
        tsq = sA.n
        Wt("vector", sA, tsq)
        E("vector", lambda e: e.reciprocal(r_w[0:1, 512:1024], r_w[0:1, 0:512]), inc=sV)
        trc = sV.n
        Wt("tensor", sV, trc)
        E("tensor", lambda e, rb=rb: e.matmul(rb, c_1f[0:1, 0:128], r_w[0:1, 512:1024],
                                              start=True, stop=True), inc=sT)
        trb = sT.n
        Wt("vector", sT, trb)
        Wt("vector", sV, t_mod)
        Wt("vector", sT, h_war)
        for ch in range(8):
            E("vector", lambda e, ch=ch, mb=mb: e.tensor_sub(xv[:, ch, :], xv[:, ch, :], mb))
            E("vector", lambda e, ch=ch, rb=rb: e.tensor_mul(xv[:, ch, :], xv[:, ch, :], rb))
            E("vector", lambda e, ch=ch: e.tensor_scalar(
                out=hv[:, ch, :], in0=xv[:, ch, :],
                scalar1=c_mod[:, ca_c + ch:ca_c + ch + 1],
                scalar2=c_mod[:, sa_c + ch:sa_c + ch + 1],
                op0=OP.mult, op1=OP.add), inc=sV)
        return sV.n

    t_kvq_hist = [0, 0]
    for tt in range(8):
        hb = tt % 2
        hta = hta2[hb]
        Wt("sync", sV, t_h_hist[0])
        t_x = dma("sync", sLD, xst, xfm[:, :, 512 * tt:512 * (tt + 1)])
        Wt("vector", sLD, t_x)
        Wt("vector", sT, t_stats_prev)
        Wt("tensor", sV, t_cval)
        if debug:
            Wt("vector", sLD, t_dbg_h)
        t_h = ln_block(xst, hta, (PB2[0:1, 0:512], PB2[0:1, 512:1024]),
                       PB2[:, 1024:1536], PB2[:, 1536:2048], 8, 0, t_kvq_hist[hb])
        t_h_hist[0] = t_h
        t_stats_prev = sT.n
        if debug:
            Wt("sync", sV, t_h)
            t_dbg_h = dma("sync", sLD, dbg_hd[:, :, 512 * tt:512 * (tt + 1)], hta)

        Wt("tensor", sV, t_h)
        Wt("tensor", sGP, t_wv if tt >= 4 else t_wq)
        Wt("tensor", sV, t_vones)
        kq_bank = [PB[:, 1024:1536], PB[:, 1536:2048]]
        for kvch in range(8):
            bk = kq_bank[kvch % 2]
            Wt("tensor", sV, kq_free[kvch % 2])
            for ch in range(8):
                E("tensor", lambda e, kvch=kvch, ch=ch, bk=bk, hta=hta: e.matmul(
                    bk, wkv[:, ch, 128 * kvch:128 * (kvch + 1)], hta[:, ch, :],
                    start=(ch == 0), stop=(ch == 7)), inc=sT)
            tkm = sT.n
            Wt("vector", sT, tkm)
            Wt("vector", sSC, t_kst)
            Wt("vector", sLD, t_F)
            E("vector", lambda e, kvch=kvch, bk=bk: e.tensor_copy(kbuf[:, kvch, :], bk),
              inc=sV)
            kq_free[kvch % 2] = sV.n
        t_kev = sV.n
        Wt("scalar", sV, t_kev)
        t_kst = dma("scalar", sSC, Kd[:, :, 512 * tt:512 * (tt + 1)], kbuf)
        for tsub in range(4):
            Wt("tensor", sV, t_mod)
            Wt("tensor", sV, t_vev)
            for ch in range(8):
                for nn in range(2):
                    E("tensor", lambda e, tsub=tsub, ch=ch, nn=nn, hta=hta: e.matmul(
                        PB[:, 512 * nn:512 * (nn + 1)],
                        hta[:, ch, 128 * tsub:128 * (tsub + 1)],
                        wvv[:, ch, 512 * nn:512 * (nn + 1)],
                        start=(ch == 0), stop=(ch == 7)), inc=sT)
            tvm = sT.n
            Wt("vector", sT, tvm)
            Wt("vector", sSC, t_vst)
            pbh = PB[:, 0:1024].rearrange("p (pr par y) -> p pr par y", par=2, y=64)
            E("vector", lambda e, tsub=tsub, pbh=pbh: e.tensor_copy(
                vbuf_pr[:, tsub, :, 0:64], pbh[:, :, 0, :]))
            E("vector", lambda e, tsub=tsub, pbh=pbh: e.tensor_copy(
                vbuf_pr[:, tsub, :, 65:129], pbh[:, :, 1, :]), inc=sV)
            t_vev = sV.n
        Wt("scalar", sV, t_vev)
        t_vst = dma("scalar", sSC, Vd[:, 4 * tt:4 * tt + 4, :], vbuf)
        if tt < 4:
            for kvch in range(8):
                bk = kq_bank[kvch % 2]
                Wt("tensor", sV, kq_free[kvch % 2])
                for ch in range(8):
                    E("tensor", lambda e, kvch=kvch, ch=ch, bk=bk, hta=hta: e.matmul(
                        bk, wqv[:, ch, 128 * kvch:128 * (kvch + 1)], hta[:, ch, :],
                        start=(ch == 0), stop=(ch == 7)), inc=sT)
                tqm = sT.n
                Wt("vector", sT, tqm)
                E("vector", lambda e, kvch=kvch, bk=bk, tt=tt: e.tensor_copy(
                    qtv[:, kvch, 512 * tt:512 * (tt + 1)], bk), inc=sV)
                kq_free[kvch % 2] = sV.n
        t_kvq_hist[hb] = sT.n
    t_kvq = sT.n

    t_phAv = sV.n
    t_phAT = sT.n
    t_phA_kst, t_phA_vst = t_kst, t_vst

    wtiles = ([("wg", "wgq", r, 1024) for r in range(22)] +
              [("wu", "wuq", r, 1024) for r in range(22)] +
              [("wd", "wdq", r, NI) for r in range(8)])
    ws_stage = XS[:, 0:1408]
    ws_u = XS[:, 1408:2816]
    ws_C = XS[:, 2816:4224]
    ws_bm = XS[:, 4224:4400]
    ws_sc = XS[:, 4400:4576]
    ws_ri = XS[:, 4576:4752]
    ws_out = XB[:, 0:2816]

    t_wq_ld = t_wq_qd = t_wq_st = t_Wlast = 0

    def quant_sub(src_ap, dst_ap, w, u_ap, C_ap, bm_ap, sc_ap, ri_ap):
        nb = w // 16
        src3 = src_ap.rearrange("p (b k) -> p b k", k=16)
        E("vector", lambda e: e.reduce_max(
            bm_ap[:, 0:nb].rearrange("p (b o) -> p b o", o=1), src3,
            axis=AX.X, apply_absolute_value=True), inc=sV)
        Wt("vector", sV, sV.n)
        E("vector", lambda e: e.tensor_scalar(out=sc_ap[:, 0:nb], in0=bm_ap[:, 0:nb],
                                              scalar1=1.0 / 6.0, scalar2=1e-12,
                                              op0=OP.mult, op1=OP.max), inc=sV)
        Wt("vector", sV, sV.n)
        E("vector", lambda e: e.reciprocal(ri_ap[:, 0:nb], sc_ap[:, 0:nb]), inc=sV)
        Wt("vector", sV, sV.n)
        u3 = u_ap[:, 0:w].rearrange("p (b k) -> p b k", k=16)
        E("vector", lambda e: e.tensor_tensor(
            u3, src3, ri_ap[:, 0:nb].rearrange("p (b o) -> p b o", o=1)
            .broadcast_to([128, nb, 16]), OP.mult))
        E("vector", lambda e: e.tensor_scalar(
            out=C_ap[:, 0:w].bitcast(I32), in0=u_ap[:, 0:w].bitcast(I32),
            scalar1=0x7F800000, scalar2=0x00400000, op0=OP.bitwise_and,
            op1=OP.bitwise_or), inc=sV)
        Wt("vector", sV, sV.n)
        E("vector", lambda e: e.tensor_scalar(
            out=C_ap[:, 0:w], in0=C_ap[:, 0:w], scalar1=1.5, scalar2=4194304.0,
            op0=OP.max, op1=OP.mult))
        E("vector", lambda e: e.tensor_add(u_ap[:, 0:w], u_ap[:, 0:w], C_ap[:, 0:w]))
        E("vector", lambda e: e.tensor_sub(u_ap[:, 0:w], u_ap[:, 0:w], C_ap[:, 0:w]))
        E("vector", lambda e: e.tensor_tensor(
            dst_ap.rearrange("p (b k) -> p b k", k=16), u3,
            sc_ap[:, 0:nb].rearrange("p (b o) -> p b o", o=1).broadcast_to([128, nb, 16]),
            OP.mult), inc=sV)
        return sV.n

    def emit_w_slice(idx0, idx1):
        nonlocal t_wq_ld, t_wq_qd, t_wq_st, t_Wlast
        Wt("gpsimd", sV, t_phAv)
        for it in range(idx0, min(idx1, len(wtiles))):
            srcn, dstn, r, wfull = wtiles[it]
            nh = (wfull + 1407) // 1408
            for hh in range(nh):
                w = min(1408, wfull - 1408 * hh)
                Wt("gpsimd", sV, t_wq_qd)
                src = di[srcn].rearrange("(a p) k -> p a k", p=128)[:, r,
                                                                   1408 * hh:1408 * hh + w]
                t_wq_ld = dma("gpsimd", sGP, ws_stage[:, 0:w], src)
                Wt("vector", sGP, t_wq_ld)
                Wt("vector", sGP, t_wq_st)
                t_wq_qd = quant_sub(ws_stage[:, 0:w], ws_out[:, 0:w], w,
                                    ws_u, ws_C, ws_bm, ws_sc, ws_ri)
                Wt("gpsimd", sV, t_wq_qd)
                dst = scr[dstn].rearrange("(a p) k -> p a k", p=128)[:, r,
                                                                    1408 * hh:1408 * hh + w]
                t_wq_st = dma("gpsimd", sGP, dst, ws_out[:, 0:w])
        t_Wlast = t_wq_st

    wov = W3[:, 0:8192].rearrange("p (c n) -> p c n", n=1024)
    kpair = [W3[:, 8192:12288], W3[:, 12288:16384]]
    slabs = [W3[:, 16384:19200], W3[:, 19200:22016]]
    vpair = HHB[:, 0:5120].rearrange("p (a c) -> p a c", c=160)
    attnT = [HHB[:, 5120:6144], HHB[:, 6144:7168]]
    obuf = HHB[:, 7168:7680]
    rdx = XS[:, 7680:8192]
    ohv = OHB[:].rearrange("p (c t) -> p c t", t=HALF)

    Wt("vector", sT, t_kvq)
    E("vector", lambda e: e.memset(vpair[:, :, 65:96], 0.0), inc=sV)
    t_vz = sV.n

    Wt("gpsimd", sT, t_kvq)
    t_wo = dma("gpsimd", sGP, W3[:, 0:8192].rearrange("p (c n) -> p c n", n=1024), di["wo"].rearrange("(c p) n -> p c n", p=128))

    slots = [PB[:, 0:1024], PB[:, 1024:2048]]
    NSLOT = 2
    slot_free = [0] * NSLOT
    att_free = [0, 0]
    pouts = [[PB2[:, 0:512], PB2[:, 1024:1536]],
             [PB2[:, 512:1024], PB2[:, 1536:2048]]]
    t_OH_pq = [[0, 0], [0, 0]]
    t_kp = [0, 0]
    t_kp_mm = [t_phAT, t_phAT]
    t_vp = 0
    t_vp_mm = t_phAT
    t_sl = [0, 0]
    t_sl_mm = [t_phAT, t_phAT]
    exp_tok = 0
    pending_fin = []
    rdxb = XB[:, 3000:3512]

    def make_fin(par, hch, qb, t_av):
        den_row = 64 if par == 0 else 32
        num0 = 0 if par == 0 else 64
        pob = pouts[par][qb % 2]

        def fin():
            Wt("vector", sT, t_av)
            E("vector", lambda e: e.reciprocal(
                rdx[den_row:den_row + 1, :], pob[den_row:den_row + 1, :]), inc=sV)
            Wt("vector", sV, sV.n)
            E("vector", lambda e: e.tensor_copy(
                rdxb[den_row:den_row + 1, :], rdx[den_row:den_row + 1, :]), inc=sV)
            trd = sV.n
            Wt("scalar", sT, t_av)
            E("scalar", lambda e: e.copy(
                obuf[num0:num0 + 64, 0:512], pob[num0:num0 + 64, :]), inc=sA)
            tob = sA.n
            Wt("tensor", sV, trd)
            Wt("tensor", sA, tob)
            E("tensor", lambda e: e.matmul(
                pob[num0:num0 + 64, :],
                c_1b[den_row:den_row + 1, 0:64], rdxb[den_row:den_row + 1, :],
                start=True, stop=True), inc=sT)
            tbc = sT.n
            Wt("vector", sT, tbc)
            Wt("vector", sA, tob)
            E("vector", lambda e: e.tensor_tensor(
                ohv[num0:num0 + 64, hch, 512 * qb:512 * (qb + 1)],
                obuf[num0:num0 + 64, 0:512],
                pob[num0:num0 + 64, :], OP.mult), inc=sV)
            t_OH_pq[par][qb % 2] = sV.n
        return fin

    n_w = len(wtiles)
    wper = (n_w + 31) // 32
    wchunk = [0]

    for hp in range(8):
        kb = hp % 2
        Wt("sync", sT, t_kp_mm[kb])
        Wt("sync", sSC, t_phA_kst)
        t_kp[kb] = dma("sync", sLD, kpair[kb], Kdp[:, hp, :])
        Wt("sync", sT, t_vp_mm)
        Wt("sync", sSC, t_phA_vst)
        Wt("sync", sV, t_vz)
        dma("sync", sLD, vpair[:, :, 0:65], Vd[:, :, 129 * hp:129 * hp + 65])
        t_vp = dma("sync", sLD, vpair[:, :, 96:160], Vd[:, :, 129 * hp + 65:129 * (hp + 1)])
        for par in range(2):
            h = 2 * hp + par
            Wt("scalar", sT, t_sl_mm[par])
            Wt("scalar", sLD, t_F)
            dma("scalar", sSC, slabs[par][:, 0:1152],
                bass.AP(scr["Flo"].tensor, h * W_LO + 1, [[1, 128], [128, 9], [1, 128]]))
            dma("scalar", sSC, slabs[par][:, 1152:2304],
                bass.AP(scr["Fhi"].tensor, h * W_LO + 1, [[1, 128], [128, 9], [1, 128]]))
            t_sl[par] = dma("scalar", sSC, slabs[par][:, 2304:2816],
                            bass.AP(scr["Fwr"].tensor, h * W_WRAP + 129,
                                    [[1, 128], [128, 4], [1, 128]]))

        hch = hp
        Wt("scalar", sV, t_cval)

        def av_emit(sm, qb):
            pm, pkm = sm % 2, sm // 2
            po = pouts[pm][qb % 2]
            Wt("tensor", sA, exp_tok)
            if pkm == 0:
                Wt("tensor", sV, t_OH_pq[pm][qb % 2])
            for jj2 in (2 * pkm, 2 * pkm + 1):
                E("tensor", lambda e, jj2=jj2, pkm=pkm, pm=pm, po=po:
                  e.matmul(po[0:65, :] if pm == 0 else po[0:128, :],
                           vpair[:, jj2, 0:65] if pm == 0 else vpair[:, jj2, 32:160],
                           attnT[sm % 2][:, 512 * (jj2 - 2 * pkm):
                                         512 * (jj2 - 2 * pkm) + 512],
                           start=(jj2 == 0), stop=(jj2 == 31)), inc=sT)
            att_free[sm % 2] = sT.n

        for qb in range(4):
            jset = _straddle_set(qb)
            Wt("tensor", sLD, t_kp[kb])
            Wt("tensor", sLD, t_vp)
            Wt("tensor", sSC, t_sl[1])
            Wt("tensor", sV, t_phAv)
            t_av_par = [0, 0]
            for step in range(32):
                par = step % 2
                pk = step // 2
                h = 2 * hp + par
                hrow = par * 64
                sl = step % NSLOT
                Wt("tensor", sA, slot_free[sl])
                for jj, off in ((2 * pk, 0), (2 * pk + 1, 512)):
                    pv = slots[sl][:, off:off + 512]
                    strad = jj in jset
                    E("tensor", lambda e, jj=jj, pv=pv, kb=kb, hrow=hrow, hch=hch, qb=qb:
                      e.matmul(pv, kpair[kb][hrow:hrow + 64, 128 * jj:128 * (jj + 1)],
                               qtv[hrow:hrow + 64, hch, 512 * qb:512 * (qb + 1)],
                               start=True, stop=(not strad),
                               tile_position=(hrow, 0)), inc=sT)
                    if strad:
                        for qp in range(4):
                            if jj == 31 and qb == 0:
                                mi = 2304 + 128 * (3 - qp)
                            else:
                                base = 0 if jj < 16 else 1152
                                mi = base + 128 * (jj - 4 * qb - qp + 4)
                            E("tensor", lambda e, pv=pv, par=par, mi=mi, qp=qp:
                              e.matmul(pv[:, 128 * qp:128 * (qp + 1)],
                                       slabs[par][:, mi:mi + 128], c_rv[:],
                                       start=False, stop=(qp == 3)), inc=sT)
                tsc = sT.n
                if step in (1, 2) and pending_fin:
                    pending_fin.pop(0)()
                if step == 3:
                    emit_w_slice(wchunk[0], wchunk[0] + wper)
                    wchunk[0] += wper
                if step >= 1:
                    av_emit(step - 1, qb)
                    if step - 1 == 30:
                        t_av_par[0] = sT.n
                Wt("scalar", sT, tsc)
                Wt("scalar", sT, att_free[step % 2])
                j0, j1 = 2 * pk, 2 * pk + 1
                s0, s1 = j0 in jset, j1 in jset
                calls = ([(0, 1024, s0)] if s0 == s1 else [(0, 512, s0), (512, 1024, s1)])
                for (o0, o1, st_) in calls:
                    bias = c_k[:, 0:1] if st_ else c_cv[:, (qb * 16 + pk) * 16 + h:
                                                        (qb * 16 + pk) * 16 + h + 1]
                    E("scalar", lambda e, ab=step % 2, sl=sl, o0=o0, o1=o1, bias=bias:
                      e.activation(attnT[ab][:, o0:o1], slots[sl][:, o0:o1],
                                   AF.Exp, bias=bias), inc=sA)
                exp_tok = sA.n
                slot_free[sl] = exp_tok
            av_emit(31, qb)
            t_av_par[1] = sT.n
            pending_fin.append(make_fin(0, hch, qb, t_av_par[0]))
            pending_fin.append(make_fin(1, hch, qb, t_av_par[1]))
        t_sl_mm[0] = t_sl_mm[1] = sT.n
        t_kp_mm[kb] = sT.n
        t_vp_mm = sT.n
    while pending_fin:
        pending_fin.pop(0)()
    emit_w_slice(wchunk[0], n_w)

    xown = XS[:, 0:4096].rearrange("p (c f) -> p c f", f=512)
    x1v = XS[:, 4096:8192].rearrange("p (c f) -> p c f", f=512)
    x1bf = XB[:, 0:4096].rearrange("p (c f) -> p c f", f=512)
    sq2 = XB[:, 4096:8192].rearrange("p (c f) -> p c f", f=512)
    h2tm2 = [KV[:, 0:1024], KV[:, 1024:2048]]
    xq1tm2 = [KV[:, 2048:3072], KV[:, 3072:4096]]
    c1u = KV[:, 4096:6144].bitcast(F32)
    c1C = KV[:, 6144:8192].bitcast(F32)
    c1bm = KV[:, 8192:8320].bitcast(F32)
    c1sc = KV[:, 8320:8448].bitcast(F32)
    c1ri = KV[:, 8448:8576].bitcast(F32)
    t_c1q_h = [0, 0]
    t_c1st_h = [0, 0]
    t_h2st = 0
    t_x1st = 0
    t_res = 0
    Wt("tensor", sA, sA.n)

    for qb in range(4):
        Wt("sync", sV, max(max(t_OH_pq[0]), max(t_OH_pq[1])))
        Wt("sync", sV, t_wq_qd)
        Wt("sync", sGP, t_wq_st)
        t_xo = dma("sync", sLD, xown, xfm[:, :, 512 * qb:512 * (qb + 1)])
        res_tok = {0: 0, 1: 0}
        for dm in range(8):
            pob = PB2[:, 1024 + 512 * (dm % 2):1536 + 512 * (dm % 2)]
            Wt("tensor", sV, max(max(t_OH_pq[0]), max(t_OH_pq[1])))
            Wt("tensor", sGP, t_wo)
            Wt("tensor", sV, res_tok[dm % 2])
            for ch in range(8):
                E("tensor", lambda e, dm=dm, ch=ch, qb=qb, pob=pob: e.matmul(
                    pob, wov[:, ch, 128 * dm:128 * (dm + 1)],
                    ohv[:, ch, 512 * qb:512 * (qb + 1)],
                    start=(ch == 0), stop=(ch == 7)), inc=sT)
            tom = sT.n
            Wt("vector", sT, tom)
            Wt("vector", sLD, t_xo)
            E("vector", lambda e, dm=dm, pob=pob: e.scalar_tensor_tensor(
                x1v[:, dm, :], pob, c_mod[:, 16 + dm:17 + dm], xown[:, dm, :],
                OP.mult, OP.add), inc=sV)
            res_tok[dm % 2] = sV.n
        t_res = sV.n
        Wt("sync", sV, t_res)
        t_x1st = dma("sync", sLD, x1d[:, :, 512 * qb:512 * (qb + 1)], x1v)
        Wt("vector", sLD, t_x1st)
        Wt("vector", sGP, t_wq_st)
        Wt("vector", sLD, t_h2st)
        t_h2 = ln_block(x1v, x1bf, (PB[0:1, 0:512], PB[0:1, 512:1024]),
                        PB[:, 1024:1536], PB[:, 1536:2048], 32, 24, 0)
        Wt("sync", sV, t_h2)
        t_h2st = dma("sync", sLD, h2d[:, :, 512 * qb:512 * (qb + 1)], x1bf)
        for j in range(4):
            tch = 4 * qb + j
            b = j % 2
            Wt("sync", sLD, t_h2st)
            Wt("sync", sV, t_c1q_h[b])
            Wt("sync", sSC, t_phA_vst)
            tld = dma_t("sync", sLD, h2tm2[b], scr["h2"][:, 128 * tch:128 * (tch + 1)])
            Wt("vector", sLD, tld)
            Wt("vector", sSC, t_c1st_h[b])
            t_c1q_h[b] = quant_sub(h2tm2[b], xq1tm2[b], 1024, c1u, c1C, c1bm, c1sc, c1ri)
            Wt("scalar", sV, t_c1q_h[b])
            t_c1st_h[b] = dma("scalar", sSC, xq1d[:, tch, :], xq1tm2[b])

    t_Olast = sT.n
    t_c1q = max(t_c1q_h)
    t_c1st = max(t_c1st_h)

    t_xq1fm = 0
    for ch in range(8):
        Wt("sync", sT, t_Olast)
        Wt("sync", sV, max(max(t_OH_pq[0]), max(t_OH_pq[1])))
        Wt("sync", sSC, t_c1st)
        t_xq1fm = dma_t("sync", sLD, ohv[:, ch, :], scr["xq1"][:, 128 * ch:128 * (ch + 1)])
    cu = XS[:, 0:1024]
    cC = XS[:, 1024:2048]
    cbm = XB[:, 2560:2912].bitcast(F32)
    csc = XB[:, 2912:3264].bitcast(F32)
    cri = XB[:, 3264:3616].bitcast(F32)

    gslab2 = [KV[:, 0:4096].rearrange("p (c f) -> p c f", f=512),
              W3[:, 0:4096].rearrange("p (c f) -> p c f", f=512)]
    uslab2 = [KV[:, 4096:8192].rearrange("p (c f) -> p c f", f=512),
              W3[:, 4096:8192].rearrange("p (c f) -> p c f", f=512)]
    hmid_f = XS[:, 2048:3072]
    silu_f = XS[:, 3072:4096]
    hq_o = XB[:, 2048:3072]
    c3bm = XB[:, 3072:3200].bitcast(F32)
    c3sc = XB[:, 3200:3328].bitcast(F32)
    c3ri = XB[:, 3328:3456].bitcast(F32)
    NSLAB = [512, 512, 512, 512, 512, 256]
    t_c3q = 0
    t_c3st = 0
    t_c3st_s = [0] * 6
    t_gm = {0: 0, 1: 0}
    t_slabmm2 = [t_kvq, t_Olast]
    tu2 = [0, 0]
    for ns in range(6):
        Wn = NSLAB[ns]
        n0 = 512 * ns
        sbuf_i = (ns + 1) % 2
        gslab, uslab = gslab2[sbuf_i], uslab2[sbuf_i]
        for ch in range(8):
            Wt("sync", sT, t_slabmm2[sbuf_i])
            if sbuf_i == 0:
                Wt("sync", sSC, t_phA_vst)
                Wt("sync", sV, t_c1q)
                Wt("sync", sSC, t_c1st)
            Wt("sync", sGP, t_Wlast)
            dma_t("sync", sLD, gslab[:, ch, 0:Wn],
                  scr["wgq"][n0:n0 + Wn, 128 * ch:128 * (ch + 1)])
            tu2[sbuf_i] = dma_t("sync", sLD, uslab[:, ch, 0:Wn],
                                scr["wuq"][n0:n0 + Wn, 128 * ch:128 * (ch + 1)])
        for tp in range(8):
            Bk = PB if tp % 2 == 0 else PB2
            t0c = 2 * tp
            Wt("tensor", sLD, tu2[sbuf_i])
            Wt("tensor", sLD, t_xq1fm)
            Wt("tensor", sV, t_gm[tp % 2])
            for ti in range(2):
                tch = t0c + ti
                pg = Bk[:, 512 * ti:512 * ti + 512]
                pu = Bk[:, 1024 + 512 * ti:1536 + 512 * ti]
                for ch in range(8):
                    E("tensor", lambda e, ch=ch, tch=tch, Wn=Wn, pg=pg, gslab=gslab: e.matmul(
                        pg[:, 0:Wn], ohv[:, ch, 128 * tch:128 * (tch + 1)], gslab[:, ch, 0:Wn],
                        start=(ch == 0), stop=(ch == 7)), inc=sT)
                for ch in range(8):
                    E("tensor", lambda e, ch=ch, tch=tch, Wn=Wn, pu=pu, uslab=uslab: e.matmul(
                        pu[:, 0:Wn], ohv[:, ch, 128 * tch:128 * (tch + 1)], uslab[:, ch, 0:Wn],
                        start=(ch == 0), stop=(ch == 7)), inc=sT)
            tgm = sT.n
            Wt("scalar", sT, tgm)
            Wt("scalar", sV, t_c3q)
            for ti in range(2):
                E("scalar", lambda e, Wn=Wn, ti=ti, Bk=Bk: e.activation(
                    silu_f[:, Wn * ti:Wn * ti + Wn], Bk[:, 512 * ti:512 * ti + Wn],
                    AF.Silu), inc=sA)
            tsil = sA.n
            Wt("vector", sA, tsil)
            for ti in range(2):
                E("vector", lambda e, Wn=Wn, ti=ti, Bk=Bk: e.tensor_tensor(
                    hmid_f[:, Wn * ti:Wn * ti + Wn], silu_f[:, Wn * ti:Wn * ti + Wn],
                    Bk[:, 1024 + 512 * ti:1024 + 512 * ti + Wn], OP.mult), inc=sV)
            t_gm[tp % 2] = sV.n
            Wt("vector", sSC, t_c3st)
            t_c3q = quant_sub(hmid_f[:, 0:2 * Wn], hq_o[:, 0:2 * Wn], 2 * Wn,
                              cu, cC, c3bm, c3sc, c3ri)
            Wt("scalar", sV, t_c3q)
            t_c3st = dma("scalar", sSC, hmidd[:, t0c:t0c + 2, n0:n0 + Wn],
                         hq_o[:, 0:2 * Wn].rearrange("p (b w) -> p b w", b=2))
        t_slabmm2[sbuf_i] = sT.n
        t_c3st_s[ns] = t_c3st

    wdqv = W3[:, 0:22528].rearrange("p (c n) -> p c n", n=1024)
    hqv2 = [QT[:, 0:11264].rearrange("p (c f) -> p c f", f=512),
            KV[:, 0:11264].rearrange("p (c f) -> p c f", f=512)]
    x2b = [XS[:, 3072:3584], XS[:, 3584:4096]]
    t_wdq = 0
    for nch in range(22):
        Wt("sync", sT, t_slabmm2[1])
        Wt("sync", sGP, t_Wlast)
        t_wdq = dma_t("sync", sLD, wdqv[:, nch, :],
                      scr["wdq"][0:1024, 128 * nch:128 * (nch + 1)])
    t_dn = {0: t_gm[1], 1: t_gm[1]}
    t_hq_war = [t_Olast, t_slabmm2[0]]
    if debug:
        Wt("gpsimd", sLD, t_wdq)
        dma("gpsimd", sGP, dbg_wdqv, W3[:, 0:22528])
    t_x2 = 0
    t_ostb = {0: 0, 1: 0}
    th_b = [0, 0]
    for tc4 in range(4):
        hb = tc4 % 2
        hqv = hqv2[hb]
        for nch in range(22):
            ns_need = nch // 4 if nch < 20 else 5
            Wt("sync", sSC, t_c3st_s[ns_need])
            Wt("sync", sT, t_hq_war[hb])
            th_b[hb] = dma_t("sync", sLD, hqv[:, nch, :],
                             scr["hmid"][512 * tc4:512 * (tc4 + 1),
                                         128 * nch:128 * (nch + 1)])
        Wt("sync", sV, t_res)
        Wt("sync", sV, t_x2)
        t_x1l = dma("sync", sLD, x1v, x1d[:, :, 512 * tc4:512 * (tc4 + 1)])
        if debug and tc4 == 3:
            Wt("gpsimd", sLD, th_b[hb])
            dma("gpsimd", sGP, dbg_hqv, KV[:, 0:11264])
        for hc in range(8):
            pd = PB2[:, 512 * (hc % 2):512 * (hc % 2) + 512]
            Wt("tensor", sLD, th_b[hb])
            Wt("tensor", sLD, t_wdq)
            Wt("tensor", sV, t_dn[hc % 2])
            for nch in range(22):
                E("tensor", lambda e, hc=hc, nch=nch, pd=pd, hqv=hqv: e.matmul(
                    pd, wdqv[:, nch, 128 * hc:128 * (hc + 1)], hqv[:, nch, :],
                    start=(nch == 0), stop=(nch == 21)), inc=sT)
            tdm = sT.n
            if debug and tc4 == 3 and hc == 0 and DBG_PD:
                Wt("scalar", sT, tdm)
                E("scalar", lambda e, pd=pd: e.activation(
                    silu_f[:, 0:512], pd, AF.Copy), inc=sA)
                Wt("gpsimd", sA, sA.n)
                dma("gpsimd", sGP, dbg_pd, silu_f[:, 0:512])
            Wt("vector", sT, tdm)
            Wt("vector", sLD, t_x1l)
            Wt("vector", sST, t_ostb[hc % 2])
            xb2 = x2b[hc % 2]
            E("vector", lambda e, hc=hc, xb2=xb2, pd=pd: e.scalar_tensor_tensor(
                xb2, pd, c_mod[:, 40 + hc:41 + hc], x1v[:, hc, :], OP.mult, OP.add), inc=sV)
            t_dn[hc % 2] = sV.n
            t_x2 = sV.n
            Wt("scalar", sV, t_x2)
            t_ostb[hc % 2] = dma("scalar", sST, outd[:, hc, 512 * tc4:512 * (tc4 + 1)], xb2)
        t_hq_war[hb] = sT.n

    Wt("sync", sST, sST.n)

    with nc.Block() as block:
        @block.sync
        def _(e):
            for fn in P.stream["sync"]:
                fn(e)

        @block.vector
        def _(e):
            for fn in P.stream["vector"]:
                fn(e)

        @block.tensor
        def _(e):
            for fn in P.stream["tensor"]:
                fn(e)

        @block.scalar
        def _(e):
            for fn in P.stream["scalar"]:
                fn(e)

        @block.gpsimd
        def _(e):
            for fn in P.stream["gpsimd"]:
                fn(e)

    es.close()
    return nc


_REVI = np.zeros((128, 128), BFNP)
for _i in range(128):
    _REVI[_i, 127 - _i] = 1.0


def kernel(L0_attn_hidden_states, L0_attn_Wq, L0_attn_Wk, L0_attn_Wv, L0_attn_Wo,
           L0_attn_rel_emb, L0_ffn_w_gate, L0_ffn_w_up, L0_ffn_w_down, L0_c, L0_wmod):
    x = np.ascontiguousarray(np.asarray(L0_attn_hidden_states, dtype=np.float32))
    rel = np.ascontiguousarray(np.asarray(L0_attn_rel_emb, dtype=np.float32))
    shared = {
        "wq": np.ascontiguousarray(np.asarray(L0_attn_Wq, np.float32)),
        "wk": np.ascontiguousarray(np.asarray(L0_attn_Wk, np.float32)),
        "wv": np.ascontiguousarray(np.asarray(L0_attn_Wv, np.float32)),
        "wo": np.ascontiguousarray(np.asarray(L0_attn_Wo, np.float32)),
        "wg": np.ascontiguousarray(np.asarray(L0_ffn_w_gate, np.float32)),
        "wu": np.ascontiguousarray(np.asarray(L0_ffn_w_up, np.float32)),
        "wd": np.ascontiguousarray(np.asarray(L0_ffn_w_down, np.float32)),
        "wmod": np.ascontiguousarray(np.asarray(L0_wmod, np.float32)),
        "revi": _REVI,
        "onesb": np.ones((128, 64), BFNP),
        "onesf": np.ones((128, 128), np.float32),
    }
    cc = np.asarray(L0_c, np.float32)
    halves = {}
    for s in (0, 1):
        flo, fhi, fwr, bmap = _bias_tables(rel, s)
        halves[s] = {"flo": flo, "fhi": fhi, "fwr": fwr, "bmap": bmap}

    in_maps = []
    for core in range(8):
        b, s = core // 2, core % 2
        perm = np.concatenate([np.arange(HALF * s, HALF * (s + 1)),
                               np.arange(HALF * (1 - s), HALF * (2 - s))])
        m = dict(shared)
        m.update(halves[s])
        m["x_fm"] = np.ascontiguousarray(x[b][perm].T)
        m["ccol"] = np.ascontiguousarray(cc[b].reshape(8, 128).T)
        in_maps.append(m)

    nc = build_program()
    res = run_bass_kernel_spmd(nc, in_maps, core_ids=list(range(8)))
    out = np.empty((4, S, D), np.float32)
    for core in range(8):
        b, s = core // 2, core % 2
        out[b, HALF * s:HALF * (s + 1)] = np.asarray(res.results[core]["out_fm"]).T
    return out
